# version 04
# baseline (speedup 1.0000x reference)
import numpy as np
import ml_dtypes
from contextlib import ExitStack
import concourse.bass as bass
import concourse.mybir as mybir
from concourse.bass_utils import run_bass_kernel_spmd

F32 = mybir.dt.float32
BF16 = mybir.dt.bfloat16
AF = mybir.ActivationFunctionType
ALU = mybir.AluOpType
AX = mybir.AxisListType

D_MODEL = 1024
D_INNER = 2048
D_FF = 2816
NK = D_MODEL // 128
NJ = D_INNER // 128
NF = D_FF // 128
NE = 8
TS = 512
FFW = TS + 3
LN_EPS = 1e-5
RMS_EPS = 1e-6


def default_cfg():
    return dict(T_PRE=12288, T_OWN=4096)


class Emitter:
    ENGINES = ("t", "v", "s", "g", "sy")

    def __init__(self, nc):
        self.nc = nc
        self.q = {e: [] for e in self.ENGINES}
        self.sems = {}
        self.cnt = {}

    def sem(self, name):
        if name not in self.sems:
            self.sems[name] = self.nc.alloc_semaphore(name=name)
            self.cnt[name] = 0
        return self.sems[name]

    def op(self, qname, fn, inc=None):
        if inc is None:
            self.q[qname].append(fn)
            return None
        sem_name, by = inc
        h = self.sem(sem_name)
        self.cnt[sem_name] += by
        after = self.cnt[sem_name]

        def wrapped(e, fn=fn, h=h, by=by):
            fn(e).then_inc(h, by)
        self.q[qname].append(wrapped)
        return after

    def wait(self, qname, sem_name, val):
        if val is None or val <= 0:
            return
        h = self.sem(sem_name)
        self.q[qname].append(lambda e, h=h, v=val: e.wait_ge(h, v))

    def dma(self, qname, out, in_, sem_name):
        self.wait(qname, sem_name, self.cnt[sem_name])
        return self.op(qname, lambda e, o=out, i=in_: e.dma_start(out=o, in_=i),
                       inc=(sem_name, 16))

    def vstep(self, fn):
        n = self.op("v", fn, inc=("vS", 1))
        self.wait("v", "vS", n)
        return n

    def sstep(self, fn):
        n = self.op("s", fn, inc=("sS", 1))
        self.wait("s", "sS", n)
        return n

    def replay(self, engine, qname):
        for fn in self.q[qname]:
            fn(engine)


def build_program(cfg):
    T_PRE = cfg["T_PRE"]
    T_OWN = cfg["T_OWN"]
    NSUB_PRE = T_PRE // TS
    NSUB_OWN = T_OWN // TS
    NT_OWN = T_OWN // 128

    nc = bass.Bass("TRN2", target_bir_lowering=False, dynamic_dma_scratch_size=16384,
                   detect_race_conditions=False)

    xpre = nc.dram_tensor("xpre", [max(T_PRE, 1), D_MODEL], F32, kind="ExternalInput")
    xown = nc.dram_tensor("xown", [8 + T_OWN, D_MODEL], F32, kind="ExternalInput")
    maskpre = nc.dram_tensor("maskpre", [1, max(T_PRE, 1)], F32, kind="ExternalInput")
    win = nc.dram_tensor("win", [D_MODEL, 2 * D_INNER], F32, kind="ExternalInput")
    wout = nc.dram_tensor("wout", [D_INNER, D_MODEL], F32, kind="ExternalInput")
    wx_r = nc.dram_tensor("wx_r", [128, NJ * 33], F32, kind="ExternalInput")
    alog_r = nc.dram_tensor("alog_r", [128, NJ * 16], F32, kind="ExternalInput")
    wdt_r = nc.dram_tensor("wdt_r", [128, NJ], F32, kind="ExternalInput")
    bdt_r = nc.dram_tensor("bdt_r", [128, NJ], F32, kind="ExternalInput")
    cw_r = nc.dram_tensor("cw_r", [128, NJ * 4], F32, kind="ExternalInput")
    cb_r = nc.dram_tensor("cb_r", [128, NJ], F32, kind="ExternalInput")
    dD_r = nc.dram_tensor("dD_r", [128, NJ], F32, kind="ExternalInput")
    normw_r = nc.dram_tensor("normw_r", [128, NK], F32, kind="ExternalInput")
    rw_r = nc.dram_tensor("rw_r", [128, NK * NE], F32, kind="ExternalInput")
    w1 = nc.dram_tensor("w1", [NE, D_MODEL, D_FF], F32, kind="ExternalInput")
    v1 = nc.dram_tensor("v1", [NE, D_MODEL, D_FF], F32, kind="ExternalInput")
    w2 = nc.dram_tensor("w2", [NE, D_FF, D_MODEL], F32, kind="ExternalInput")
    fbias = nc.dram_tensor("fbias", [1, D_MODEL], F32, kind="ExternalInput")
    ident_in = nc.dram_tensor("ident_in", [128, 128], BF16, kind="ExternalInput")

    out_t = nc.dram_tensor("out", [T_OWN, D_MODEL], F32, kind="ExternalOutput")

    dbg = cfg.get("dbg", False)
    if dbg:
        dbg_t = {
            "dbg_xp": nc.dram_tensor("dbg_xp", [128, NJ * FFW], F32,
                                     kind="ExternalOutput"),
            "dbg_xc": nc.dram_tensor("dbg_xc", [128, NJ * TS], F32,
                                     kind="ExternalOutput"),
            "dbg_dt": nc.dram_tensor("dbg_dt", [128, NJ * TS], F32,
                                     kind="ExternalOutput"),
            "dbg_carry": nc.dram_tensor("dbg_carry", [128, NJ * 16], F32,
                                        kind="ExternalOutput"),
            "dbg_ssm": nc.dram_tensor("dbg_ssm", [33, TS], F32,
                                      kind="ExternalOutput"),
            "dbg_a2t": nc.dram_tensor("dbg_a2t", [128, NK * TS], BF16,
                                      kind="ExternalOutput"),
            "dbg_y": nc.dram_tensor("dbg_y", [128, NJ * TS], F32,
                                    kind="ExternalOutput"),
            "dbg_m": nc.dram_tensor("dbg_m", [128, 64 * NE], F32,
                                    kind="ExternalOutput"),
            "dbg_sc": nc.dram_tensor("dbg_sc", [128, 64 * NE], F32,
                                     kind="ExternalOutput"),
            "dbg_s1": nc.dram_tensor("dbg_s1", [128, 64], F32,
                                     kind="ExternalOutput"),
            "dbg_b1": nc.dram_tensor("dbg_b1", [128, 64], F32,
                                     kind="ExternalOutput"),
            "dbg_a1t": nc.dram_tensor("dbg_a1t", [128, NK * TS], BF16,
                                      kind="ExternalOutput"),
            "dbg_oi": nc.dram_tensor("dbg_oi", [128, 4 * D_MODEL], F32,
                                     kind="ExternalOutput"),
            "dbg_asb": nc.dram_tensor("dbg_asb", [128, NJ * 16], F32,
                                      kind="ExternalOutput"),
            "dbg_abxh": nc.dram_tensor("dbg_abxh", [128, 3 * TS], F32,
                                       kind="ExternalOutput"),
        }
    ygb_scr = nc.dram_tensor("ygb_scr", [NJ, 128, T_OWN], BF16, kind="Internal")
    ssm_dram = nc.dram_tensor("ssm_dram", [33, TS], F32, kind="Internal")
    hb_scr = nc.dram_tensor("hb_scr", [NE, NF, 128, T_OWN], BF16, kind="Internal")

    E = Emitter(nc)
    for s in ("dS", "dA", "dG", "dB0", "dB1", "dV", "dH0", "dH1",
              "tS", "vS", "sS", "pS", "dY"):
        E.sem(s)

    es0 = ExitStack()
    sb0 = lambda n, sh, dt=F32: es0.enter_context(nc.sbuf_tensor(n, sh, dt))

    ident = sb0("ident", [128, 128], BF16)
    a_sb = sb0("a_sb", [128, NJ * 16], F32)
    wdt_sb = sb0("wdt_sb", [128, NJ], F32)
    bdt_sb = sb0("bdt_sb", [128, NJ], F32)
    cw_sb = sb0("cw_sb", [128, NJ * 4], F32)
    cb_sb = sb0("cb_sb", [128, NJ], F32)
    dD_sb = sb0("dD_sb", [128, NJ], F32)
    normw_sb = sb0("normw_sb", [128, NK], F32)
    rw_sb = sb0("rw_sb", [128, NK * NE], BF16)
    carry = sb0("carry", [128, NJ * 16], F32)
    s1_all = sb0("s1_all", [128, NT_OWN], F32)
    b1_all = sb0("b1_all", [128, NT_OWN], F32)
    m_sb = sb0("m_sb", [128, NT_OWN * NE], F32)
    sc_sb = sb0("sc_sb", [128, NT_OWN * NE], F32)
    st = sb0("st", [128, 64], F32)
    eps_sb = sb0("eps_sb", [128, 3], F32)
    tmp_buf = sb0("tmp_buf", [128, TS], F32)

    E.dma("sy", ident[:, :], ident_in.ap(), "dS")
    E.dma("sy", a_sb[:, :], alog_r.ap(), "dS")
    E.dma("sy", wdt_sb[:, :], wdt_r.ap(), "dS")
    E.dma("sy", bdt_sb[:, :], bdt_r.ap(), "dS")
    E.dma("sy", cw_sb[:, :], cw_r.ap(), "dS")
    E.dma("sy", cb_sb[:, :], cb_r.ap(), "dS")
    E.dma("sy", dD_sb[:, :], dD_r.ap(), "dS")
    d_setup = E.dma("sy", normw_sb[:, :], normw_r.ap(), "dS")
    E.dma("g", rw_sb[:, :], rw_r.ap(), "dG")
    d_gsetup0 = E.cnt["dG"]

    E.op("v", lambda e: e.memset(eps_sb[:, 0:1], LN_EPS))
    E.op("v", lambda e: e.memset(eps_sb[:, 1:2], RMS_EPS))
    v_eps = E.op("v", lambda e: e.memset(eps_sb[:, 2:3], -0.5), inc=("vS", 1))
    E.wait("s", "dS", d_setup)
    E.wait("s", "vS", v_eps)
    s_a1 = E.op("s", lambda e: e.activation(a_sb[:, :], a_sb[:, :], AF.Exp),
                inc=("sS", 1))
    E.wait("v", "sS", s_a1)
    E.vstep(lambda e: e.memset(carry[:, :], 0.0))

    es1 = ExitStack()
    sb1 = lambda n, sh, dt=F32: es1.enter_context(nc.sbuf_tensor(n, sh, dt))
    ps1 = lambda n, sh, dt=F32: es1.enter_context(nc.psum_tensor(n, sh, dt))

    win_x = sb1("win_x", [128, NK * D_INNER], BF16)
    wx_sb = sb1("wx_sb", [128, NJ * 33], BF16)
    x_sb = sb1("x_sb", [128, D_MODEL], F32)
    a2_sb = sb1("a2_sb", [128, D_MODEL], BF16)
    a2t = sb1("a2t", [128, NK * TS], BF16)
    xp_sb = sb1("xp_sb", [128, NJ * FFW], F32)
    xc_sb = sb1("xc_sb", [128, NJ * TS], F32)
    dt_sb = sb1("dt_sb", [128, NJ * TS], F32)
    y_sb = sb1("y_sb", [128, NJ * TS], F32)
    tail_sb = sb1("tail_sb", [128, NJ * 3], F32)
    ssm_sb = sb1("ssm_sb", [33, TS], F32)
    r_bc = sb1("r_bc", [128, TS], F32)
    b_bc = sb1("b_bc", [128, 2 * TS], F32)
    c_bc = sb1("c_bc", [128, 2 * TS], F32)
    mask_bc = c_bc
    a_buf = sb1("a_buf", [128, 2 * TS], F32)
    bx_buf = sb1("bx_buf", [128, 2 * TS], F32)
    h_buf = sb1("h_buf", [128, TS], F32)
    xcb_buf = sb1("xcb_buf", [128, TS], BF16)
    zpan = sb1("zpan", [128, NK * 128], BF16)

    tp_ps = [ps1(f"tp_ps{i}", [128, 1024], BF16) for i in range(2)]
    mm_ps = [ps1(f"mm_ps{i}", [128, 512], F32) for i in range(2)]
    z_ps = [ps1(f"z_ps{i}", [128, 512], F32) for i in range(2)]
    ssm_ps = ps1("ssm_ps", [33, TS], F32)
    pad_ps = ps1("pad_ps", [128, 512], F32)

    E.dma("g", wx_sb[:, :], wx_r.ap(), "dG")
    d_gsetup = E.dma("g", win_x[:, :].rearrange("p (k c) -> p k c", k=NK),
                     win.ap()[:, 0:D_INNER].rearrange("(k p) c -> p k c", p=128),
                     "dG")
    vv = E.op("v", lambda e: e.memset(xp_sb[:, :], 0.0), inc=("vS", 1))
    E.wait("t", "dG", d_gsetup)
    E.wait("t", "vS", vv)

    state = dict(mm_war=[None, None], z_war=[None, None], tp_war=[None, None],
                 abuf_war=[None, None], a2t_use_t=0, a2sb_use_t=0,
                 ygst_war=0, xsb_use=("vS", 0), ssm_bc0=0, ssm_bc1=0,
                 ssm_sd=0, wx_mm=0,
                 rbc_use=0, maskbc_use=0)

    def c_(i, k):
        return st[:, 14 * i + k:14 * i + k + 1]

    def load_stats_a2_transpose(own, k, xsrc, xoff):
        sev = None
        for i in range(4):
            E.wait("sy", *state["xsb_use"])
            d_x = E.dma("sy", x_sb[:, :],
                        xsrc.ap()[xoff + i * 128:xoff + (i + 1) * 128, :], "dS")
            E.wait("v", "dS", d_x)
            xt = x_sb[:, :]
            E.op("v", lambda e, xt=xt, i=i: e.reduce_sum(c_(i, 0), xt, axis=AX.X))
            E.op("v", lambda e, xt=xt: e.tensor_tensor(
                tmp_buf[:, :], xt[:, 0:512], xt[:, 0:512], op=ALU.mult))
            E.vstep(lambda e, i=i: e.reduce_sum(c_(i, 1), tmp_buf[:, :], axis=AX.X))
            E.op("v", lambda e, xt=xt: e.tensor_tensor(
                tmp_buf[:, :], xt[:, 512:1024], xt[:, 512:1024], op=ALU.mult))
            E.vstep(lambda e, i=i: e.reduce_sum(c_(i, 2), tmp_buf[:, :], axis=AX.X))
            E.vstep(lambda e, i=i: e.tensor_scalar(
                out=c_(i, 3), in0=c_(i, 0), scalar1=1.0 / D_MODEL, scalar2=None,
                op0=ALU.mult))
            E.vstep(lambda e, i=i: e.tensor_tensor(c_(i, 4), c_(i, 1), c_(i, 2),
                                                   op=ALU.add))
            E.vstep(lambda e, i=i: e.tensor_scalar(
                out=c_(i, 4), in0=c_(i, 4), scalar1=1.0 / D_MODEL, scalar2=None,
                op0=ALU.mult))
            E.vstep(lambda e, i=i: e.tensor_tensor(c_(i, 5), c_(i, 3), c_(i, 3),
                                                   op=ALU.mult))
            lv = E.vstep(lambda e, i=i: e.tensor_tensor(c_(i, 6), c_(i, 4), c_(i, 5),
                                                        op=ALU.subtract))
            E.wait("s", "vS", lv)
            E.sstep(lambda e, i=i: e.activation(c_(i, 7), c_(i, 6), AF.Ln,
                                                bias=eps_sb[:, 0:1]))
            sl = E.sstep(lambda e, i=i: e.activation(c_(i, 8), c_(i, 7), AF.Exp,
                                                     scale=eps_sb[:, 2:3]))
            E.wait("v", "sS", sl)
            E.vstep(lambda e, i=i: e.tensor_tensor(c_(i, 9), c_(i, 8), c_(i, 8),
                                                   op=ALU.mult))
            lv = E.vstep(lambda e, i=i: e.tensor_tensor(c_(i, 9), c_(i, 6), c_(i, 9),
                                                        op=ALU.mult))
            E.wait("s", "vS", lv)
            E.sstep(lambda e, i=i: e.activation(c_(i, 10), c_(i, 9), AF.Ln,
                                                bias=eps_sb[:, 1:2]))
            sl = E.sstep(lambda e, i=i: e.activation(c_(i, 11), c_(i, 10), AF.Exp,
                                                     scale=eps_sb[:, 2:3]))
            E.wait("v", "sS", sl)
            E.vstep(lambda e, i=i: e.tensor_tensor(c_(i, 12), c_(i, 8), c_(i, 11),
                                                   op=ALU.mult))
            lv = E.vstep(lambda e, i=i: e.tensor_tensor(c_(i, 13), c_(i, 3), c_(i, 12),
                                                        op=ALU.mult))
            lv = E.vstep(lambda e, i=i: e.tensor_scalar(
                out=c_(i, 13), in0=c_(i, 13), scalar1=-1.0, scalar2=None,
                op0=ALU.mult))
            if own:
                bt = k * 4 + i
                E.vstep(lambda e, i=i, bt=bt: e.tensor_copy(
                    s1_all[:, bt:bt + 1], c_(i, 8)))
                E.vstep(lambda e, i=i: e.tensor_tensor(
                    st[:, 60:61], c_(i, 3), c_(i, 8), op=ALU.mult))
                lv = E.vstep(lambda e, i=i, bt=bt: e.tensor_scalar(
                    out=b1_all[:, bt:bt + 1], in0=st[:, 60:61], scalar1=-1.0,
                    scalar2=None, op0=ALU.mult))
            E.wait("v", "tS", state["a2sb_use_t"])
            sl = E.op("v", lambda e, i=i: e.tensor_scalar(
                out=a2_sb[:, :], in0=x_sb[:, :], scalar1=c_(i, 12),
                scalar2=c_(i, 13), op0=ALU.mult, op1=ALU.add), inc=("vS", 1))
            state["xsb_use"] = ("vS", sl)
            E.wait("t", "vS", sl)
            for kk in range(NK):
                bank = kk % 2
                if state["tp_war"][bank] is not None:
                    E.wait("t", "sS", state["tp_war"][bank])
                tl = E.op("t", lambda e, kk=kk, bank=bank: e.transpose(
                    tp_ps[bank][:, 0:128], a2_sb[:, kk * 128:(kk + 1) * 128],
                    ident[:, :]), inc=("tS", 1))
                state["a2sb_use_t"] = tl
                E.wait("s", "tS", tl)
                E.wait("s", "tS", state["a2t_use_t"])
                sev = E.op("s", lambda e, kk=kk, i=i, bank=bank: e.activation(
                    a2t[:, kk * TS + i * 128:kk * TS + (i + 1) * 128],
                    tp_ps[bank][:, 0:128], AF.Copy,
                    scale=normw_sb[:, kk:kk + 1]), inc=("sS", 1))
                state["tp_war"][bank] = sev
        return sev

    def mamba_subchunk(k, own, xsrc, xoff, mask_off, skip_tail):
        if mask_off is not None:
            E.wait("sy", "vS", state["maskbc_use"])
            d_mask = E.dma("sy", mask_bc[:, 0:TS],
                           maskpre.ap()[0:1, mask_off:mask_off + TS]
                           .broadcast_to([128, TS]), "dS")
        a2t_ready = load_stats_a2_transpose(own, k, xsrc, xoff)

        if not skip_tail:
            for j in range(NJ):
                E.op("v", lambda e, j=j: e.tensor_copy(
                    tail_sb[:, j * 3:(j + 1) * 3],
                    xp_sb[:, j * FFW + TS:j * FFW + TS + 3]))
            E.vstep(lambda e: e.engine_nop())
            for j in range(NJ):
                E.op("v", lambda e, j=j: e.tensor_copy(
                    xp_sb[:, j * FFW:j * FFW + 3], tail_sb[:, j * 3:(j + 1) * 3]))
        v_tail = E.op("v", lambda e: e.engine_nop(), inc=("vS", 1))

        E.wait("t", "sS", a2t_ready)
        for j in range(NJ):
            war = state["mm_war"][j % 2]
            if war is not None:
                E.wait("t", "vS", war)
            mml = None
            for kk in range(NK):
                mml = E.op("t", lambda e, j=j, kk=kk: e.matmul(
                    mm_ps[j % 2][:, :],
                    win_x[:, kk * D_INNER + j * 128:kk * D_INNER + (j + 1) * 128],
                    a2t[:, kk * TS:(kk + 1) * TS],
                    start=(kk == 0), stop=(kk == NK - 1)),
                    inc=("tS", 1))
            state["a2t_use_t"] = mml
            E.wait("v", "tS", mml)
            if j == 0:
                E.wait("v", "vS", v_tail)
            ev = E.op("v", lambda e, j=j: e.tensor_copy(
                xp_sb[:, j * FFW + 3:j * FFW + 3 + TS], mm_ps[j % 2][:, :]),
                inc=("vS", 1))
            state["mm_war"][j % 2] = ev

        s_silu = None
        for j in range(NJ):
            xcj = xc_sb[:, j * TS:(j + 1) * TS]
            E.op("v", lambda e, j=j, xcj=xcj: e.tensor_scalar(
                out=xcj, in0=xp_sb[:, j * FFW:j * FFW + TS],
                scalar1=cw_sb[:, j * 4:j * 4 + 1], scalar2=None, op0=ALU.mult))
            for kk in range(1, 4):
                E.op("v", lambda e, j=j, kk=kk, xcj=xcj: e.scalar_tensor_tensor(
                    xcj, xp_sb[:, j * FFW + kk:j * FFW + kk + TS],
                    cw_sb[:, j * 4 + kk:j * 4 + kk + 1], xcj,
                    op0=ALU.mult, op1=ALU.add))
            cv = E.op("v", lambda e: e.engine_nop(), inc=("vS", 1))
            E.wait("s", "vS", cv)
            s_silu = E.op("s", lambda e, xcj=xcj, j=j: e.activation(
                xcj, xcj, AF.Silu, bias=cb_sb[:, j:j + 1]), inc=("sS", 1))
            if own:
                E.wait("v", "sS", s_silu)
                if j == 0:
                    E.wait("v", "dY", state["ygst_war"])
                E.op("v", lambda e, j=j, xcj=xcj: e.tensor_scalar(
                    out=y_sb[:, j * TS:(j + 1) * TS], in0=xcj,
                    scalar1=dD_sb[:, j:j + 1], scalar2=None, op0=ALU.mult))

        E.wait("v", "sS", s_silu)
        E.wait("v", "dG", state["ssm_sd"])
        mml = None
        for j in range(NJ):
            if j >= 1:
                E.wait("v", "tS", state["wx_mm"])
            cb = E.op("v", lambda e, j=j: e.tensor_copy(
                xcb_buf[:, :], xc_sb[:, j * TS:(j + 1) * TS]), inc=("vS", 1))
            E.wait("t", "vS", cb)
            mml = E.op("t", lambda e, j=j: e.matmul(
                ssm_ps[:, :], wx_sb[:, j * 33:(j + 1) * 33],
                xcb_buf[:, :],
                start=(j == 0), stop=(j == NJ - 1)), inc=("tS", 1))
            state["wx_mm"] = mml
        E.wait("v", "tS", mml)
        v_ssm = E.op("v", lambda e: e.tensor_copy(ssm_sb[:, :], ssm_ps[:, :]),
                     inc=("vS", 1))
        E.wait("t", "vS", v_ssm)
        E.wait("g", "vS", v_ssm)
        E.wait("g", "dB0", state["ssm_bc0"])
        E.wait("g", "dB1", state["ssm_bc1"])
        d_sd = E.dma("g", ssm_dram.ap(), ssm_sb[:, :], "dG")
        state["ssm_sd"] = d_sd
        E.wait("g", "sS", state["rbc_use"])
        d_r = E.dma("g", r_bc[:, :],
                    ssm_dram.ap()[0:1, :].broadcast_to([128, TS]), "dG")

        E.wait("s", "dG", d_r)
        v_dt = None
        for j in range(NJ):
            dtj = dt_sb[:, j * TS:(j + 1) * TS]
            E.op("s", lambda e, j=j, dtj=dtj: e.activation(
                dtj, r_bc[:, :], AF.Relu, scale=wdt_sb[:, j:j + 1],
                bias=bdt_sb[:, j:j + 1]))
            E.op("s", lambda e, j=j: e.activation(
                tmp_buf[:, :], r_bc[:, :], AF.Abs, scale=wdt_sb[:, j:j + 1],
                bias=bdt_sb[:, j:j + 1]))
            E.op("s", lambda e: e.activation(tmp_buf[:, :], tmp_buf[:, :], AF.Exp,
                                             scale=-1.0))
            sl = E.op("s", lambda e: e.activation(tmp_buf[:, :], tmp_buf[:, :], AF.Ln,
                                                  bias=1.0), inc=("sS", 1))
            E.wait("v", "sS", sl)
            if mask_off is not None and j == 0:
                E.wait("v", "dS", d_mask)
            v_dt = E.op("v", lambda e, dtj=dtj: e.tensor_tensor(
                dtj, dtj, tmp_buf[:, :], op=ALU.add), inc=("vS", 1))
            if mask_off is not None:
                v_dt = E.op("v", lambda e, dtj=dtj: e.tensor_tensor(
                    dtj, dtj, mask_bc[:, 0:TS], op=ALU.mult), inc=("vS", 1))
            else:
                v_dt = E.op("v", lambda e, dtj=dtj: e.tensor_scalar(
                    out=dtj, in0=dtj, scalar1=-1.0, scalar2=None, op0=ALU.mult),
                    inc=("vS", 1))
            E.wait("s", "vS", v_dt)
        state["rbc_use"] = E.cnt["sS"]
        state["maskbc_use"] = v_dt if mask_off is not None else state["maskbc_use"]

        v_u = None
        for j in range(NJ):
            v_u = E.op("v", lambda e, j=j: e.tensor_tensor(
                xc_sb[:, j * TS:(j + 1) * TS], xc_sb[:, j * TS:(j + 1) * TS],
                dt_sb[:, j * TS:(j + 1) * TS], op=ALU.mult), inc=("vS", 1))

        s_gz = {}
        if own:
            for j in range(NJ):
                E.op("v", lambda e, j=j: e.tensor_copy(
                    tail_sb[:, j * 3:(j + 1) * 3],
                    xp_sb[:, j * FFW + TS:j * FFW + TS + 3]))
            v_tail2 = E.op("v", lambda e: e.engine_nop(), inc=("vS", 1))
            zp_use = 0
            for m in range(NJ):
                E.wait("g", "tS", zp_use)
                d_zp = E.dma(
                    "g", zpan[:, :].rearrange("p (k c) -> p k c", k=NK),
                    win.ap()[:, D_INNER + m * 128:D_INNER + (m + 1) * 128]
                    .rearrange("(k p) c -> p k c", p=128), "dG")
                war = state["z_war"][m % 2]
                if war is not None:
                    E.wait("t", "sS", war)
                E.wait("t", "dG", d_zp)
                mml = None
                for kk in range(NK):
                    mml = E.op("t", lambda e, m=m, kk=kk: e.matmul(
                        z_ps[m % 2][:, :],
                        zpan[:, kk * 128:(kk + 1) * 128],
                        a2t[:, kk * TS:(kk + 1) * TS],
                        start=(kk == 0), stop=(kk == NK - 1)), inc=("tS", 1))
                zp_use = mml
                state["a2t_use_t"] = mml
                E.wait("s", "tS", mml)
                if m == 0:
                    E.wait("s", "vS", v_tail2)
                sg = E.op("s", lambda e, m=m: e.activation(
                    xp_sb[:, m * FFW:m * FFW + TS], z_ps[m % 2][:, :], AF.Silu),
                    inc=("sS", 1))
                state["z_war"][m % 2] = sg
                s_gz[m] = sg

        E.wait("s", "vS", v_u)
        E.wait("g", "vS", v_u)
        for sig in range(16):
            par = sig % 2
            E.wait("g", "vS", state.get(f"bbc_use{par}", 0))
            d_b = E.dma("g", b_bc[:, par * TS:(par + 1) * TS],
                        ssm_dram.ap()[1 + sig:2 + sig, :].broadcast_to([128, TS]),
                        f"dB{par}")
            if own:
                d_c = E.dma("g", c_bc[:, par * TS:(par + 1) * TS],
                            ssm_dram.ap()[17 + sig:18 + sig, :]
                            .broadcast_to([128, TS]), f"dB{par}")
            E.wait("g", f"dB{par}", d_b)
            for j in range(NJ):
                col = j * 16 + sig
                pb = (sig * NJ + j) % 2
                war = state["abuf_war"][pb]
                if war is not None:
                    E.wait("s", "vS", war)
                    E.wait("g", "vS", war)
                sa = E.op("s", lambda e, j=j, col=col, pb=pb: e.activation(
                    a_buf[:, pb * TS:(pb + 1) * TS], dt_sb[:, j * TS:(j + 1) * TS],
                    AF.Exp, scale=a_sb[:, col:col + 1]), inc=("sS", 1))
                pbx = E.op("g", lambda e, j=j, par=par, pb=pb: e.tensor_tensor(
                    bx_buf[:, pb * TS:(pb + 1) * TS],
                    xc_sb[:, j * TS:(j + 1) * TS],
                    b_bc[:, par * TS:(par + 1) * TS], op=ALU.mult), inc=("pS", 1))
                E.wait("v", "sS", sa)
                E.wait("v", "pS", pbx)
                if own and j == 0:
                    E.wait("v", f"dB{par}", d_c)
                sc = E.op("v", lambda e, col=col, pb=pb: e.tensor_tensor_scan(
                    h_buf[:, :], a_buf[:, pb * TS:(pb + 1) * TS],
                    bx_buf[:, pb * TS:(pb + 1) * TS],
                    carry[:, col:col + 1], op0=ALU.mult, op1=ALU.add),
                    inc=("vS", 1))
                state["abuf_war"][pb] = sc
                E.wait("v", "vS", sc)
                if dbg and own and sig == 0 and j == 0 and (xoff - 8) // TS == 0:
                    E.wait("s", "vS", sc)
                    E.dma("s", dbg_t["dbg_asb"].ap(), a_sb[:, :], "dA")
                    E.dma("s", dbg_t["dbg_abxh"].ap()[:, 0:TS],
                          a_buf[:, pb * TS:(pb + 1) * TS], "dA")
                    E.dma("s", dbg_t["dbg_abxh"].ap()[:, TS:2 * TS], bx_buf[:, :],
                          "dA")
                    E.dma("s", dbg_t["dbg_abxh"].ap()[:, 2 * TS:3 * TS], h_buf[:, :],
                          "dA")
                    dbgd = E.cnt["dA"]
                    E.wait("v", "dA", dbgd)
                E.op("v", lambda e, col=col: e.tensor_copy(
                    carry[:, col:col + 1], h_buf[:, TS - 1:TS]))
                if own:
                    E.op("v", lambda e, par=par: e.tensor_tensor(
                        tmp_buf[:, :], h_buf[:, :],
                        c_bc[:, par * TS:(par + 1) * TS], op=ALU.mult))
                    E.op("v", lambda e, j=j: e.tensor_tensor(
                        y_sb[:, j * TS:(j + 1) * TS], y_sb[:, j * TS:(j + 1) * TS],
                        tmp_buf[:, :], op=ALU.subtract))
            lastv = E.op("v", lambda e: e.engine_nop(), inc=("vS", 1))
            state[f"bbc_use{par}"] = lastv
        state["ssm_bc0"] = E.cnt["dB0"]
        state["ssm_bc1"] = E.cnt["dB1"]

        if own:
            ksub = (xoff - 8) // TS
            if dbg and ksub == 0:
                E.wait("s", "vS", E.cnt["vS"])
                E.wait("s", "sS", E.cnt["sS"])
                for nm, sbuf in (("dbg_xp", xp_sb), ("dbg_xc", xc_sb),
                                 ("dbg_dt", dt_sb), ("dbg_carry", carry),
                                 ("dbg_ssm", ssm_sb), ("dbg_a2t", a2t),
                                 ("dbg_y", y_sb)):
                    E.dma("s", dbg_t[nm].ap(), sbuf[:, :], "dA")
                dbg_done = E.cnt["dA"]
                for qn in ("v", "g", "sy", "t"):
                    E.wait(qn, "dA", dbg_done)
                E.wait("s", "dA", dbg_done)
            for j in range(NJ):
                E.wait("v", "sS", s_gz[j])
                gv = E.op("v", lambda e, j=j: e.tensor_tensor(
                    y_sb[:, j * TS:(j + 1) * TS], y_sb[:, j * TS:(j + 1) * TS],
                    xp_sb[:, j * FFW:j * FFW + TS], op=ALU.mult), inc=("vS", 1))
                E.wait("g", "vS", gv)
                E.dma("g", ygb_scr.ap()[j, :, ksub * TS:(ksub + 1) * TS],
                      y_sb[:, j * TS:(j + 1) * TS], "dY")
            state["ygst_war"] = E.cnt["dY"]

    for k in range(NSUB_PRE):
        mamba_subchunk(k, own=False, xsrc=xpre, xoff=k * TS, mask_off=k * TS,
                       skip_tail=(k == 0))


    E.wait("sy", *state["xsb_use"])
    d_pad = E.dma("sy", x_sb[0:8, 0:D_MODEL], xown.ap()[0:8, :], "dS")
    E.wait("v", "dS", d_pad)
    pc = lambda k: st[0:8, k:k + 1]
    E.op("v", lambda e: e.reduce_sum(pc(0), x_sb[0:8, 0:D_MODEL], axis=AX.X))
    E.op("v", lambda e: e.tensor_tensor(tmp_buf[0:8, :], x_sb[0:8, 0:512],
                                        x_sb[0:8, 0:512], op=ALU.mult))
    E.vstep(lambda e: e.reduce_sum(pc(1), tmp_buf[0:8, :], axis=AX.X))
    E.op("v", lambda e: e.tensor_tensor(tmp_buf[0:8, :], x_sb[0:8, 512:1024],
                                        x_sb[0:8, 512:1024], op=ALU.mult))
    E.vstep(lambda e: e.reduce_sum(pc(2), tmp_buf[0:8, :], axis=AX.X))
    E.vstep(lambda e: e.tensor_scalar(out=pc(3), in0=pc(0), scalar1=1.0 / D_MODEL,
                                      scalar2=None, op0=ALU.mult))
    E.vstep(lambda e: e.tensor_tensor(pc(4), pc(1), pc(2), op=ALU.add))
    E.vstep(lambda e: e.tensor_scalar(out=pc(4), in0=pc(4), scalar1=1.0 / D_MODEL,
                                      scalar2=None, op0=ALU.mult))
    E.vstep(lambda e: e.tensor_tensor(pc(5), pc(3), pc(3), op=ALU.mult))
    pv = E.vstep(lambda e: e.tensor_tensor(pc(6), pc(4), pc(5), op=ALU.subtract))
    E.wait("s", "vS", pv)
    E.sstep(lambda e: e.activation(pc(7), pc(6), AF.Ln, bias=eps_sb[0:8, 0:1]))
    sx = E.sstep(lambda e: e.activation(pc(8), pc(7), AF.Exp,
                                        scale=eps_sb[0:8, 2:3]))
    E.wait("v", "sS", sx)
    E.vstep(lambda e: e.tensor_tensor(pc(9), pc(8), pc(8), op=ALU.mult))
    pv = E.vstep(lambda e: e.tensor_tensor(pc(9), pc(6), pc(9), op=ALU.mult))
    E.wait("s", "vS", pv)
    E.sstep(lambda e: e.activation(pc(10), pc(9), AF.Ln, bias=eps_sb[0:8, 1:2]))
    sx = E.sstep(lambda e: e.activation(pc(11), pc(10), AF.Exp,
                                        scale=eps_sb[0:8, 2:3]))
    E.wait("v", "sS", sx)
    E.vstep(lambda e: e.tensor_tensor(pc(12), pc(8), pc(11), op=ALU.mult))
    E.vstep(lambda e: e.tensor_tensor(pc(13), pc(3), pc(12), op=ALU.mult))
    pv = E.vstep(lambda e: e.tensor_scalar(out=pc(13), in0=pc(13), scalar1=-1.0,
                                           scalar2=None, op0=ALU.mult))
    E.wait("v", "tS", state["a2sb_use_t"])
    sx = E.op("v", lambda e: e.tensor_scalar(
        out=a2_sb[0:8, 0:D_MODEL], in0=x_sb[0:8, 0:D_MODEL], scalar1=pc(12),
        scalar2=pc(13), op0=ALU.mult, op1=ALU.add), inc=("vS", 1))
    state["xsb_use"] = ("vS", sx)
    E.wait("t", "vS", sx)
    tl = None
    for kk in range(NK):
        war = state["tp_war"][0]
        if war is not None and kk == 0:
            E.wait("t", "sS", war)
        tl = E.op("t", lambda e, kk=kk: e.transpose(
            tp_ps[0][:, kk * 8:(kk + 1) * 8], a2_sb[0:8, kk * 128:(kk + 1) * 128],
            ident[0:8, 0:8]), inc=("tS", 1))
    state["a2sb_use_t"] = tl
    E.wait("s", "tS", tl)
    E.wait("s", "tS", state["a2t_use_t"])
    sx = None
    for kk in range(NK):
        sx = E.op("s", lambda e, kk=kk: e.activation(
            a2t[:, kk * TS:kk * TS + 8], tp_ps[0][:, kk * 8:(kk + 1) * 8],
            AF.Copy, scale=normw_sb[:, kk:kk + 1]), inc=("sS", 1))
    state["tp_war"][0] = sx
    E.wait("t", "sS", sx)
    pvl = None
    for j in range(NJ):
        if pvl is not None:
            E.wait("t", "vS", pvl)
        mml = None
        for kk in range(NK):
            mml = E.op("t", lambda e, j=j, kk=kk: e.matmul(
                pad_ps[:, 0:8],
                win_x[:, kk * D_INNER + j * 128:kk * D_INNER + (j + 1) * 128],
                a2t[:, kk * TS:kk * TS + 8],
                start=(kk == 0), stop=(kk == NK - 1)), inc=("tS", 1))
        E.wait("v", "tS", mml)
        pvl = E.op("v", lambda e, j=j: e.tensor_copy(
            xp_sb[:, j * FFW:j * FFW + 3], pad_ps[:, 5:8]), inc=("vS", 1))
    state["a2t_use_t"] = E.cnt["tS"]

    for k in range(NSUB_OWN):
        mamba_subchunk(k, own=True, xsrc=xown, xoff=8 + k * TS, mask_off=None,
                       skip_tail=(k == 0))


    E.wait("t", "vS", E.cnt["vS"])
    E.wait("t", "sS", E.cnt["sS"])
    E.wait("t", "dA", E.cnt["dA"])
    fence_t = E.op("t", lambda e: e.nop(), inc=("tS", 1))
    for qn in ("v", "s", "g", "sy"):
        E.wait(qn, "tS", fence_t)
    es1.close()

    es2 = ExitStack()
    sb2 = lambda n, sh, dt=F32: es2.enter_context(nc.sbuf_tensor(n, sh, dt))
    ps2 = lambda n, sh, dt=F32: es2.enter_context(nc.psum_tensor(n, sh, dt))

    x2_sb = sb2("x2_sb", [128, 4 * D_MODEL], F32)
    a22_sb = sb2("a22_sb", [128, 4 * D_MODEL], BF16)
    a1t = sb2("a1t", [128, NK * T_OWN], BF16)
    gbuf = sb2("gbuf", [128, 2 * TS], F32)
    hbst = sb2("hbst", [128, 2 * 8 * TS], BF16)
    w1p = sb2("w1p", [128, 2 * NK * 128], BF16)
    v1p = sb2("v1p", [128, 2 * NK * 128], BF16)
    tp2_ps = [ps2(f"tp2_ps{i}", [128, 1024], BF16) for i in range(2)]
    h_ps = [ps2(f"h_ps{i}", [128, TS], F32) for i in range(4)]
    r_ps = ps2("r_ps", [128, 512], F32)

    tp2_war = [None, None]
    x2_use = 0
    for k in range(NSUB_OWN):
        E.wait("sy", "vS", x2_use if x2_use else 0)
        d_x = E.dma("sy", x2_sb[:, :].rearrange("p (i d) -> p i d", i=4),
                    xown.ap()[8 + k * TS:8 + (k + 1) * TS, :]
                    .rearrange("(i p) d -> p i d", p=128), "dS")
        E.wait("v", "dS", d_x)
        sa = None
        for i in range(4):
            tt = k * 4 + i
            sa = E.op("v", lambda e, i=i, tt=tt: e.tensor_scalar(
                out=a22_sb[:, i * D_MODEL:(i + 1) * D_MODEL],
                in0=x2_sb[:, i * D_MODEL:(i + 1) * D_MODEL],
                scalar1=s1_all[:, tt:tt + 1], scalar2=b1_all[:, tt:tt + 1],
                op0=ALU.mult, op1=ALU.add), inc=("vS", 1))
        x2_use = sa
        E.wait("t", "vS", sa)
        for kk in range(NK):
            if tp2_war[kk % 2] is not None:
                E.wait("t", "sS", tp2_war[kk % 2])
            tl = None
            for i in range(4):
                tl = E.op("t", lambda e, kk=kk, i=i: e.transpose(
                    tp2_ps[kk % 2][:, i * 128:(i + 1) * 128],
                    a22_sb[:, i * D_MODEL + kk * 128:i * D_MODEL + (kk + 1) * 128],
                    ident[:, :]), inc=("tS", 1))
            E.wait("s", "tS", tl)
            sev = E.op("s", lambda e, kk=kk, k=k: e.activation(
                a1t[:, kk * T_OWN + k * TS:kk * T_OWN + (k + 1) * TS],
                tp2_ps[kk % 2][:, 0:512], AF.Copy), inc=("sS", 1))
            tp2_war[kk % 2] = sev
    a1t_ready = E.cnt["sS"]

    E.wait("t", "sS", a1t_ready)
    rp_use = 0
    for g4 in range(NT_OWN // 4):
        E.wait("t", "vS", rp_use)
        mml = None
        for i in range(4):
            tt = g4 * 4 + i
            for kk in range(NK):
                mml = E.op("t", lambda e, tt=tt, kk=kk, i=i: e.matmul(
                    r_ps[:, i * NE:(i + 1) * NE],
                    a1t[:, kk * T_OWN + tt * 128:kk * T_OWN + (tt + 1) * 128],
                    rw_sb[:, kk * NE:(kk + 1) * NE],
                    start=(kk == 0), stop=(kk == NK - 1)), inc=("tS", 1))
        E.wait("v", "tS", mml)
        rp_use = E.op("v", lambda e, g4=g4: e.tensor_copy(
            sc_sb[:, g4 * 4 * NE:(g4 + 1) * 4 * NE], r_ps[:, 0:4 * NE]),
            inc=("vS", 1))
    for tt in range(NT_OWN):
        p = sc_sb[:, tt * NE:(tt + 1) * NE]
        mx = st[:, 0:1]
        E.vstep(lambda e, p=p, mx=mx: e.reduce_max(mx, p, axis=AX.X))
        E.vstep(lambda e, p=p, mx=mx: e.tensor_scalar(
            out=p, in0=p, scalar1=mx, scalar2=None, op0=ALU.subtract))
    E.wait("s", "vS", E.cnt["vS"])
    sxx = E.op("s", lambda e: e.activation(sc_sb[:, 0:NT_OWN * NE],
                                           sc_sb[:, 0:NT_OWN * NE], AF.Exp),
               inc=("sS", 1))
    E.wait("s", "sS", sxx)
    E.wait("v", "sS", sxx)
    for tt in range(NT_OWN):
        p = sc_sb[:, tt * NE:(tt + 1) * NE]
        m = m_sb[:, tt * NE:(tt + 1) * NE]
        sm = st[:, 1:2]
        m1 = st[:, 2:3]
        m2 = st[:, 3:4]
        E.vstep(lambda e, p=p, sm=sm: e.reduce_sum(sm, p, axis=AX.X))
        E.vstep(lambda e, sm=sm: e.reciprocal(sm, sm))
        E.vstep(lambda e, p=p, sm=sm: e.tensor_scalar(
            out=p, in0=p, scalar1=sm, scalar2=None, op0=ALU.mult))
        E.vstep(lambda e, p=p, m1=m1: e.reduce_max(m1, p, axis=AX.X))
        E.vstep(lambda e, p=p, m=m, m1=m1: e.tensor_scalar(
            out=m, in0=p, scalar1=m1, scalar2=1e30, op0=ALU.is_ge, op1=ALU.mult))
        E.vstep(lambda e, p=p, m=m: e.tensor_tensor(m, p, m, op=ALU.subtract))
        E.vstep(lambda e, m=m, m2=m2: e.reduce_max(m2, m, axis=AX.X))
        E.vstep(lambda e, p=p, m=m, m2=m2: e.tensor_scalar(
            out=m, in0=p, scalar1=m2, scalar2=None, op0=ALU.is_ge))
        E.vstep(lambda e, p=p, m=m: e.tensor_tensor(m, p, m, op=ALU.mult))

    if dbg:
        E.wait("s", "vS", E.cnt["vS"])
        E.dma("s", dbg_t["dbg_m"].ap()[:, 0:NT_OWN * NE], m_sb[:, :], "dA")
        E.dma("s", dbg_t["dbg_sc"].ap()[:, 0:NT_OWN * NE], sc_sb[:, :], "dA")
        E.dma("s", dbg_t["dbg_s1"].ap()[:, 0:NT_OWN], s1_all[:, :], "dA")
        E.dma("s", dbg_t["dbg_b1"].ap()[:, 0:NT_OWN], b1_all[:, :], "dA")
        E.dma("s", dbg_t["dbg_a1t"].ap(), a1t[:, 0:NK * TS], "dA")
        dbg_done = E.cnt["dA"]
        for qn in ("v", "g", "sy", "t"):
            E.wait(qn, "dA", dbg_done)
        E.wait("s", "dA", dbg_done)
    E.wait("t", "vS", E.cnt["vS"])
    panel_use = {}
    hps_war_v = [None, None]
    hps_war_s = [None, None]
    hbst_dma = [0, 0]
    for ee in range(NE):
        for fb in range(NF):
            it = ee * NF + fb
            pp = it % 2
            E.wait("g", "tS", panel_use.get(pp, 0))
            E.dma("g", w1p[:, pp * NK * 128:(pp + 1) * NK * 128]
                  .rearrange("p (k f) -> p k f", k=NK),
                  w1.ap()[ee, :, fb * 128:(fb + 1) * 128]
                  .rearrange("(k p) f -> p k f", p=128), "dG")
            d_w1v = E.cnt["dG"]
            d_v1 = E.dma("g", v1p[:, pp * NK * 128:(pp + 1) * NK * 128]
                         .rearrange("p (k f) -> p k f", k=NK),
                         v1.ap()[ee, :, fb * 128:(fb + 1) * 128]
                         .rearrange("(k p) f -> p k f", p=128), "dV")
            E.wait("t", "dG", d_w1v)
            E.wait("t", "dV", d_v1)
            hoff = (it % 2) * 8 * TS
            E.wait("v", f"dH{it % 2}", hbst_dma[it % 2])
            mml = None
            for tsb in range(T_OWN // TS):
                pb = tsb % 2
                if hps_war_s[pb] is not None:
                    E.wait("t", "sS", hps_war_s[pb])
                    E.wait("t", "vS", hps_war_v[pb])
                for kk in range(NK):
                    E.op("t", lambda e, pp=pp, kk=kk, tsb=tsb, pb=pb: e.matmul(
                        h_ps[2 * pb][:, :],
                        w1p[:, pp * NK * 128 + kk * 128:pp * NK * 128 + (kk + 1) * 128],
                        a1t[:, kk * T_OWN + tsb * TS:kk * T_OWN + (tsb + 1) * TS],
                        start=(kk == 0), stop=(kk == NK - 1)),
                        inc=("tS", 1) if kk == NK - 1 else None)
                m1l = E.cnt["tS"]
                for kk in range(NK):
                    E.op("t", lambda e, pp=pp, kk=kk, tsb=tsb, pb=pb: e.matmul(
                        h_ps[2 * pb + 1][:, :],
                        v1p[:, pp * NK * 128 + kk * 128:pp * NK * 128 + (kk + 1) * 128],
                        a1t[:, kk * T_OWN + tsb * TS:kk * T_OWN + (tsb + 1) * TS],
                        start=(kk == 0), stop=(kk == NK - 1)),
                        inc=("tS", 1) if kk == NK - 1 else None)
                m2l = E.cnt["tS"]
                mml = m2l
                E.wait("s", "tS", m1l)
                if hps_war_v[pb] is not None:
                    E.wait("s", "vS", hps_war_v[pb])
                sg = E.op("s", lambda e, pb=pb: e.activation(
                    gbuf[:, pb * TS:(pb + 1) * TS], h_ps[2 * pb][:, :], AF.Gelu),
                    inc=("sS", 1))
                hps_war_s[pb] = sg
                E.wait("v", "tS", m2l)
                E.wait("v", "sS", sg)
                hv = E.op("v", lambda e, pb=pb, tsb=tsb, hoff=hoff: e.tensor_tensor(
                    hbst[:, hoff + tsb * TS:hoff + (tsb + 1) * TS],
                    gbuf[:, pb * TS:(pb + 1) * TS], h_ps[2 * pb + 1][:, :],
                    op=ALU.mult), inc=("vS", 1))
                hps_war_v[pb] = hv
            panel_use[pp] = mml
            E.wait("s", "vS", E.cnt["vS"])
            hbst_dma[it % 2] = E.dma("s", hb_scr.ap()[ee, fb, :, :],
                                     hbst[:, hoff:hoff + T_OWN],
                                     f"dH{it % 2}")

    E.wait("t", "vS", E.cnt["vS"])
    E.wait("t", "sS", E.cnt["sS"])
    E.wait("t", "dA", E.cnt["dA"])
    fence_t2 = E.op("t", lambda e: e.nop(), inc=("tS", 1))
    for qn in ("v", "s", "g", "sy"):
        E.wait(qn, "tS", fence_t2)
    es2.close()

    es3 = ExitStack()
    sb3 = lambda n, sh, dt=F32: es3.enter_context(nc.sbuf_tensor(n, sh, dt))
    ps3 = lambda n, sh, dt=F32: es3.enter_context(nc.psum_tensor(n, sh, dt))

    out_acc = sb3("out_acc", [128, NT_OWN * D_MODEL], F32)
    bias_bc = sb3("bias_bc", [128, D_MODEL], F32)
    s1p_all = sb3("s1p_all", [128, NT_OWN], F32)
    o_ps = [ps3(f"o_ps{i}", [128, 512], F32) for i in range(4)]

    es3a = ExitStack()
    x3_sb = es3a.enter_context(nc.sbuf_tensor("x3_sb", [128, 4 * D_MODEL], F32))
    ygt = es3a.enter_context(nc.sbuf_tensor("ygt", [128, NJ * TS], BF16))
    wout_sb = es3a.enter_context(nc.sbuf_tensor("wout_sb", [128, NJ * D_MODEL], BF16))

    E.vstep(lambda e: e.tensor_scalar(out=s1p_all[:, :], in0=s1_all[:, :],
                                      scalar1=1.0, scalar2=None, op0=ALU.add))
    d_wo = E.dma("g", wout_sb[:, :].rearrange("p (k c) -> p k c", k=NJ),
                 wout.ap().rearrange("(k p) c -> p k c", p=128), "dG")
    d_bb = E.dma("g", bias_bc[:, :],
                 fbias.ap()[0:1, :].broadcast_to([128, D_MODEL]), "dG")
    E.wait("t", "dG", d_wo)
    ygt_use = 0
    ops_use = [0, 0, 0, 0]
    for qd in range(T_OWN // TS):
        E.wait("sy", "vS", E.cnt["vS"])
        d_x = E.dma("sy", x3_sb[:, :].rearrange("p (i d) -> p i d", i=4),
                    xown.ap()[8 + qd * TS:8 + (qd + 1) * TS, :]
                    .rearrange("(i p) d -> p i d", p=128), "dS")
        dy = E.dma("sy", ygt[:, :].rearrange("p (j t) -> p j t", j=NJ),
                   ygb_scr.ap()[:, :, qd * TS:(qd + 1) * TS]
                   .rearrange("j p t -> p j t"), "dS")
        E.wait("v", "dS", d_x)
        E.wait("v", "dG", d_bb)
        for i in range(4):
            tt = qd * 4 + i
            E.op("v", lambda e, i=i, tt=tt: e.tensor_scalar(
                out=out_acc[:, tt * D_MODEL:(tt + 1) * D_MODEL],
                in0=x3_sb[:, i * D_MODEL:(i + 1) * D_MODEL],
                scalar1=s1p_all[:, tt:tt + 1], scalar2=b1_all[:, tt:tt + 1],
                op0=ALU.mult, op1=ALU.add))
            E.op("v", lambda e, i=i, tt=tt: e.tensor_tensor(
                out_acc[:, tt * D_MODEL:(tt + 1) * D_MODEL],
                out_acc[:, tt * D_MODEL:(tt + 1) * D_MODEL], bias_bc[:, :],
                op=ALU.add))
        vinit = E.op("v", lambda e: e.engine_nop(), inc=("vS", 1))
        E.wait("t", "dS", dy)
        E.wait("t", "vS", vinit)
        for i in range(4):
            tt = qd * 4 + i
            for half in range(2):
                pidx = 2 * (i % 2) + half
                E.wait("t", "vS", ops_use[pidx])
                mml = None
                for j in range(NJ):
                    mml = E.op("t", lambda e, j=j, i=i, half=half, pidx=pidx: e.matmul(
                        o_ps[pidx][:, :],
                        ygt[:, j * TS + i * 128:j * TS + (i + 1) * 128],
                        wout_sb[:, j * D_MODEL + half * 512:
                                j * D_MODEL + (half + 1) * 512],
                        start=(j == 0), stop=(j == NJ - 1)), inc=("tS", 1))
                E.wait("v", "tS", mml)
                vv = E.op("v", lambda e, tt=tt, half=half, pidx=pidx: e.tensor_tensor(
                    out_acc[:, tt * D_MODEL + half * 512:
                            tt * D_MODEL + (half + 1) * 512],
                    out_acc[:, tt * D_MODEL + half * 512:
                            tt * D_MODEL + (half + 1) * 512],
                    o_ps[pidx][:, :], op=ALU.add), inc=("vS", 1))
                ops_use[pidx] = vv
    if dbg:
        E.wait("s", "vS", E.cnt["vS"])
        E.dma("s", dbg_t["dbg_oi"].ap(), out_acc[:, 0:4 * D_MODEL], "dA")
        dbg_done = E.cnt["dA"]
        for qn in ("v", "g", "sy", "t"):
            E.wait(qn, "dA", dbg_done)
        E.wait("s", "dA", dbg_done)
    E.wait("t", "vS", E.cnt["vS"])
    fence_t3 = E.op("t", lambda e: e.nop(), inc=("tS", 1))
    for qn in ("v", "s", "g", "sy"):
        E.wait(qn, "tS", fence_t3)
    es3a.close()

    w2_sb = sb3("w2_sb", [128, NF * D_MODEL], BF16)
    hbq = sb3("hbq", [128, 2 * NF * 256], BF16)
    w2_use = 0
    hbq_use = [0, 0]
    for ee in range(NE):
        E.wait("g", "tS", w2_use)
        d_w2 = E.dma("g", w2_sb[:, :].rearrange("p (f c) -> p f c", f=NF),
                     w2.ap()[ee, :, :].rearrange("(f p) c -> p f c", p=128), "dG")
        E.wait("t", "dG", d_w2)
        for ch in range(T_OWN // 256):
            pb = ch % 2
            E.wait("sy", "tS", hbq_use[pb])
            d_hb = E.dma("sy",
                         hbq[:, pb * NF * 256:(pb + 1) * NF * 256]
                         .rearrange("p (f t) -> p f t", f=NF),
                         hb_scr.ap()[ee, :, :, ch * 256:(ch + 1) * 256]
                         .rearrange("f p t -> p f t"), "dS")
            E.wait("t", "dS", d_hb)
            for i2 in range(2):
                tt = ch * 2 + i2
                for half in range(2):
                    pidx = 2 * i2 + half
                    E.wait("t", "vS", ops_use[pidx])
                    mml = None
                    for fb in range(NF):
                        mml = E.op("t", lambda e, fb=fb, i2=i2, half=half, pb=pb,
                                   pidx=pidx: e.matmul(
                            o_ps[pidx][:, :],
                            hbq[:, pb * NF * 256 + fb * 256 + i2 * 128:
                                pb * NF * 256 + fb * 256 + (i2 + 1) * 128],
                            w2_sb[:, fb * D_MODEL + half * 512:
                                  fb * D_MODEL + (half + 1) * 512],
                            start=(fb == 0), stop=(fb == NF - 1)), inc=("tS", 1))
                    E.wait("v", "tS", mml)
                    vv = E.op("v", lambda e, tt=tt, half=half, ee=ee, pidx=pidx:
                              e.scalar_tensor_tensor(
                        out_acc[:, tt * D_MODEL + half * 512:
                                tt * D_MODEL + (half + 1) * 512],
                        o_ps[pidx][:, :],
                        m_sb[:, tt * NE + ee:tt * NE + ee + 1],
                        out_acc[:, tt * D_MODEL + half * 512:
                                tt * D_MODEL + (half + 1) * 512],
                        op0=ALU.mult, op1=ALU.add), inc=("vS", 1))
                    ops_use[pidx] = vv
            hbq_use[pb] = E.cnt["tS"]
        w2_use = E.cnt["tS"]

    E.wait("s", "vS", E.cnt["vS"])
    for tt in range(NT_OWN):
        E.dma("s", out_t.ap()[tt * 128:(tt + 1) * 128, :],
              out_acc[:, tt * D_MODEL:(tt + 1) * D_MODEL], "dA")
    E.wait("s", "dA", E.cnt["dA"])
    E.wait("g", "dA", E.cnt["dA"])
    E.wait("sy", "dS", E.cnt["dS"])
    E.wait("sy", "dA", E.cnt["dA"])

    with nc.Block() as block:
        @block.tensor
        def _(eng):
            E.replay(eng, "t")

        @block.vector
        def _(eng):
            E.replay(eng, "v")

        @block.scalar
        def _(eng):
            E.replay(eng, "s")

        @block.gpsimd
        def _(eng):
            E.replay(eng, "g")

        @block.sync
        def _(eng):
            E.replay(eng, "sy")

    es3.close()
    es0.close()
    return nc


def _tile_rows(a, nt):
    a2 = np.asarray(a, dtype=np.float32).reshape(nt, 128, -1)
    return np.ascontiguousarray(a2.transpose(1, 0, 2).reshape(128, -1))


def stage_inputs(cfg, x, norm_w, W_in, conv_w, conv_b, W_x, w_dt, b_dt, A_log,
                 D_param, W_out, router_w, w1, v1, w2, bias):
    T_PRE, T_OWN = cfg["T_PRE"], cfg["T_OWN"]
    x = np.asarray(x, dtype=np.float32)
    B, T, D = x.shape
    f32 = lambda a: np.ascontiguousarray(np.asarray(a, dtype=np.float32))
    base = {
        "win": f32(W_in), "wout": f32(W_out),
        "wx_r": _tile_rows(W_x, NJ), "alog_r": _tile_rows(A_log, NJ),
        "wdt_r": _tile_rows(np.reshape(w_dt, (-1, 1)), NJ),
        "bdt_r": _tile_rows(np.reshape(b_dt, (-1, 1)), NJ),
        "cw_r": _tile_rows(np.reshape(conv_w, (D_INNER, 4)), NJ),
        "cb_r": _tile_rows(np.reshape(conv_b, (-1, 1)), NJ),
        "dD_r": _tile_rows(np.reshape(D_param, (-1, 1)), NJ),
        "normw_r": _tile_rows(np.reshape(norm_w, (-1, 1)), NK),
        "rw_r": _tile_rows(router_w, NK),
        "w1": f32(w1), "v1": f32(v1), "w2": f32(w2),
        "fbias": f32(bias).reshape(1, D_MODEL),
        "ident_in": np.eye(128, dtype=np.float32).astype(ml_dtypes.bfloat16),
    }
    in_maps = []
    for c in range(8):
        b, q = c // 4, c % 4
        t0 = q * T_OWN
        mask = np.zeros((1, max(T_PRE, 1)), dtype=np.float32)
        mask[0, :min(t0, T_PRE)] = -1.0
        prev8 = x[b, t0 - 8:t0] if t0 >= 8 else np.zeros((8, D), dtype=np.float32)
        xo = np.ascontiguousarray(np.concatenate([prev8, x[b, t0:t0 + T_OWN]], axis=0))
        m = dict(base)
        m["xpre"] = np.ascontiguousarray(x[b, :T_PRE]) if T_PRE > 0 else \
            np.zeros((1, D), dtype=np.float32)
        m["xown"] = xo
        m["maskpre"] = mask
        in_maps.append(m)
    return in_maps


def kernel(L0_attn_x, L0_attn_norm_w, L0_attn_W_in, L0_attn_conv_w, L0_attn_conv_b,
           L0_attn_W_x, L0_attn_w_dt, L0_attn_b_dt, L0_attn_A_log, L0_attn_D_param,
           L0_attn_W_out, L0_ffn_router_w, L0_ffn_w1, L0_ffn_v1, L0_ffn_w2,
           L0_ffn_bias):
    cfg = default_cfg()
    x = np.asarray(L0_attn_x, dtype=np.float32)
    B, T, D = x.shape
    nc = build_program(cfg)
    in_maps = stage_inputs(cfg, x, L0_attn_norm_w, L0_attn_W_in, L0_attn_conv_w,
                           L0_attn_conv_b, L0_attn_W_x, L0_attn_w_dt, L0_attn_b_dt,
                           L0_attn_A_log, L0_attn_D_param, L0_attn_W_out,
                           L0_ffn_router_w, L0_ffn_w1, L0_ffn_v1, L0_ffn_w2,
                           L0_ffn_bias)
    res = run_bass_kernel_spmd(nc, in_maps, core_ids=list(range(8)))
    T_OWN = cfg["T_OWN"]
    out = np.empty((B, T, D), dtype=np.float32)
    for c in range(8):
        b, q = c // 4, c % 4
        out[b, q * T_OWN:(q + 1) * T_OWN] = res.results[c]["out"]
    return out


# version 05
# speedup vs baseline: 1.1057x; 1.1057x over previous
import numpy as np
import ml_dtypes
from contextlib import ExitStack
import concourse.bass as bass
import concourse.mybir as mybir
from concourse.bass_utils import run_bass_kernel_spmd

F32 = mybir.dt.float32
BF16 = mybir.dt.bfloat16
AF = mybir.ActivationFunctionType
ALU = mybir.AluOpType
AX = mybir.AxisListType

D_MODEL = 1024
D_INNER = 2048
D_FF = 2816
NK = D_MODEL // 128
NJ = D_INNER // 128
NF = D_FF // 128
NE = 8
TS = 512
FFW = TS + 3
LN_EPS = 1e-5
RMS_EPS = 1e-6


def default_cfg():
    return dict(T_PRE=12288, T_OWN=4096)


class Emitter:
    ENGINES = ("t", "v", "s", "g", "sy")

    def __init__(self, nc):
        self.nc = nc
        self.q = {e: [] for e in self.ENGINES}
        self.sems = {}
        self.cnt = {}

    def sem(self, name):
        if name not in self.sems:
            self.sems[name] = self.nc.alloc_semaphore(name=name)
            self.cnt[name] = 0
        return self.sems[name]

    def op(self, qname, fn, inc=None):
        if inc is None:
            self.q[qname].append(fn)
            return None
        sem_name, by = inc
        h = self.sem(sem_name)
        self.cnt[sem_name] += by
        after = self.cnt[sem_name]

        def wrapped(e, fn=fn, h=h, by=by):
            fn(e).then_inc(h, by)
        self.q[qname].append(wrapped)
        return after

    def wait(self, qname, sem_name, val):
        if val is None or val <= 0:
            return
        h = self.sem(sem_name)
        self.q[qname].append(lambda e, h=h, v=val: e.wait_ge(h, v))

    def dma(self, qname, out, in_, sem_name):
        self.wait(qname, sem_name, self.cnt[sem_name])
        return self.op(qname, lambda e, o=out, i=in_: e.dma_start(out=o, in_=i),
                       inc=(sem_name, 16))

    def vstep(self, fn):
        n = self.op("v", fn, inc=("vS", 1))
        self.wait("v", "vS", n)
        return n

    def sstep(self, fn):
        n = self.op("s", fn, inc=("sS", 1))
        self.wait("s", "sS", n)
        return n

    def replay(self, engine, qname):
        for fn in self.q[qname]:
            fn(engine)


def build_program(cfg):
    T_PRE = cfg["T_PRE"]
    T_OWN = cfg["T_OWN"]
    NSUB_PRE = T_PRE // TS
    NSUB_OWN = T_OWN // TS
    NT_OWN = T_OWN // 128

    nc = bass.Bass("TRN2", target_bir_lowering=False, dynamic_dma_scratch_size=16384,
                   detect_race_conditions=False)

    xpre = nc.dram_tensor("xpre", [max(T_PRE, 1), D_MODEL], F32, kind="ExternalInput")
    xown = nc.dram_tensor("xown", [8 + T_OWN, D_MODEL], F32, kind="ExternalInput")
    maskpre = nc.dram_tensor("maskpre", [1, max(T_PRE, 1)], F32, kind="ExternalInput")
    win = nc.dram_tensor("win", [D_MODEL, 2 * D_INNER], F32, kind="ExternalInput")
    wout = nc.dram_tensor("wout", [D_INNER, D_MODEL], F32, kind="ExternalInput")
    wx_r = nc.dram_tensor("wx_r", [128, NJ * 33], F32, kind="ExternalInput")
    alog_r = nc.dram_tensor("alog_r", [128, NJ * 16], F32, kind="ExternalInput")
    wdt_r = nc.dram_tensor("wdt_r", [128, NJ], F32, kind="ExternalInput")
    bdt_r = nc.dram_tensor("bdt_r", [128, NJ], F32, kind="ExternalInput")
    cw_r = nc.dram_tensor("cw_r", [128, NJ * 4], F32, kind="ExternalInput")
    cb_r = nc.dram_tensor("cb_r", [128, NJ], F32, kind="ExternalInput")
    dD_r = nc.dram_tensor("dD_r", [128, NJ], F32, kind="ExternalInput")
    normw_r = nc.dram_tensor("normw_r", [128, NK], F32, kind="ExternalInput")
    rw_r = nc.dram_tensor("rw_r", [128, NK * NE], F32, kind="ExternalInput")
    w1 = nc.dram_tensor("w1", [NE, D_MODEL, D_FF], F32, kind="ExternalInput")
    v1 = nc.dram_tensor("v1", [NE, D_MODEL, D_FF], F32, kind="ExternalInput")
    w2 = nc.dram_tensor("w2", [NE, D_FF, D_MODEL], F32, kind="ExternalInput")
    fbias = nc.dram_tensor("fbias", [1, D_MODEL], F32, kind="ExternalInput")
    ident_in = nc.dram_tensor("ident_in", [128, 128], BF16, kind="ExternalInput")

    out_t = nc.dram_tensor("out", [T_OWN, D_MODEL], F32, kind="ExternalOutput")

    dbg = cfg.get("dbg", False)
    if dbg:
        dbg_t = {
            "dbg_xp": nc.dram_tensor("dbg_xp", [128, NJ * FFW], F32,
                                     kind="ExternalOutput"),
            "dbg_xc": nc.dram_tensor("dbg_xc", [128, NJ * TS], F32,
                                     kind="ExternalOutput"),
            "dbg_dt": nc.dram_tensor("dbg_dt", [128, NJ * TS], F32,
                                     kind="ExternalOutput"),
            "dbg_carry": nc.dram_tensor("dbg_carry", [128, NJ * 16], F32,
                                        kind="ExternalOutput"),
            "dbg_ssm": nc.dram_tensor("dbg_ssm", [33, TS], F32,
                                      kind="ExternalOutput"),
            "dbg_a2t": nc.dram_tensor("dbg_a2t", [128, NK * TS], BF16,
                                      kind="ExternalOutput"),
            "dbg_y": nc.dram_tensor("dbg_y", [128, NJ * TS], F32,
                                    kind="ExternalOutput"),
            "dbg_m": nc.dram_tensor("dbg_m", [128, 64 * NE], F32,
                                    kind="ExternalOutput"),
            "dbg_sc": nc.dram_tensor("dbg_sc", [128, 64 * NE], F32,
                                     kind="ExternalOutput"),
            "dbg_s1": nc.dram_tensor("dbg_s1", [128, 64], F32,
                                     kind="ExternalOutput"),
            "dbg_b1": nc.dram_tensor("dbg_b1", [128, 64], F32,
                                     kind="ExternalOutput"),
            "dbg_a1t": nc.dram_tensor("dbg_a1t", [128, NK * TS], BF16,
                                      kind="ExternalOutput"),
            "dbg_oi": nc.dram_tensor("dbg_oi", [128, 4 * D_MODEL], F32,
                                     kind="ExternalOutput"),
            "dbg_asb": nc.dram_tensor("dbg_asb", [128, NJ * 16], F32,
                                      kind="ExternalOutput"),
            "dbg_abxh": nc.dram_tensor("dbg_abxh", [128, 3 * TS], F32,
                                       kind="ExternalOutput"),
        }
    ygb_scr = nc.dram_tensor("ygb_scr", [NJ, 128, T_OWN], BF16, kind="Internal")
    ssm_dram = nc.dram_tensor("ssm_dram", [33, TS], F32, kind="Internal")
    hb_scr = nc.dram_tensor("hb_scr", [NE, NF, 128, T_OWN], BF16, kind="Internal")

    E = Emitter(nc)
    for s in ("dS", "dA", "dG", "dB0", "dB1", "dV", "dH0", "dH1",
              "tS", "vS", "sS", "pS", "dY"):
        E.sem(s)

    es0 = ExitStack()
    sb0 = lambda n, sh, dt=F32: es0.enter_context(nc.sbuf_tensor(n, sh, dt))

    ident = sb0("ident", [128, 128], BF16)
    a_sb = sb0("a_sb", [128, NJ * 16], F32)
    wdt_sb = sb0("wdt_sb", [128, NJ], F32)
    bdt_sb = sb0("bdt_sb", [128, NJ], F32)
    cw_sb = sb0("cw_sb", [128, NJ * 4], F32)
    cb_sb = sb0("cb_sb", [128, NJ], F32)
    dD_sb = sb0("dD_sb", [128, NJ], F32)
    normw_sb = sb0("normw_sb", [128, NK], F32)
    rw_sb = sb0("rw_sb", [128, NK * NE], BF16)
    carry = sb0("carry", [128, NJ * 16], F32)
    s1_all = sb0("s1_all", [128, NT_OWN], F32)
    b1_all = sb0("b1_all", [128, NT_OWN], F32)
    m_sb = sb0("m_sb", [128, NT_OWN * NE], F32)
    sc_sb = sb0("sc_sb", [128, NT_OWN * NE], F32)
    st = sb0("st", [128, 64], F32)
    eps_sb = sb0("eps_sb", [128, 3], F32)
    tmp_buf = sb0("tmp_buf", [128, TS], F32)

    E.dma("sy", ident[:, :], ident_in.ap(), "dS")
    E.dma("sy", a_sb[:, :], alog_r.ap(), "dS")
    E.dma("sy", wdt_sb[:, :], wdt_r.ap(), "dS")
    E.dma("sy", bdt_sb[:, :], bdt_r.ap(), "dS")
    E.dma("sy", cw_sb[:, :], cw_r.ap(), "dS")
    E.dma("sy", cb_sb[:, :], cb_r.ap(), "dS")
    E.dma("sy", dD_sb[:, :], dD_r.ap(), "dS")
    d_setup = E.dma("sy", normw_sb[:, :], normw_r.ap(), "dS")
    E.dma("g", rw_sb[:, :], rw_r.ap(), "dG")
    d_gsetup0 = E.cnt["dG"]

    E.op("v", lambda e: e.memset(eps_sb[:, 0:1], LN_EPS))
    E.op("v", lambda e: e.memset(eps_sb[:, 1:2], RMS_EPS))
    v_eps = E.op("v", lambda e: e.memset(eps_sb[:, 2:3], -0.5), inc=("vS", 1))
    E.wait("s", "dS", d_setup)
    E.wait("s", "vS", v_eps)
    s_a1 = E.op("s", lambda e: e.activation(a_sb[:, :], a_sb[:, :], AF.Exp),
                inc=("sS", 1))
    E.wait("v", "sS", s_a1)
    E.vstep(lambda e: e.memset(carry[:, :], 0.0))

    es1 = ExitStack()
    sb1 = lambda n, sh, dt=F32: es1.enter_context(nc.sbuf_tensor(n, sh, dt))
    ps1 = lambda n, sh, dt=F32: es1.enter_context(nc.psum_tensor(n, sh, dt))

    win_x = sb1("win_x", [128, NK * D_INNER], BF16)
    wx_sb = sb1("wx_sb", [128, NJ * 33], BF16)
    x_sb = sb1("x_sb", [128, D_MODEL], F32)
    a2_sb = sb1("a2_sb", [128, D_MODEL], BF16)
    a2t = sb1("a2t", [128, NK * TS], BF16)
    xp_sb = sb1("xp_sb", [128, NJ * FFW], F32)
    xc_sb = sb1("xc_sb", [128, NJ * TS], F32)
    dt_sb = sb1("dt_sb", [128, NJ * TS], F32)
    y_sb = sb1("y_sb", [128, NJ * TS], F32)
    tail_sb = sb1("tail_sb", [128, NJ * 3], F32)
    ssm_sb = sb1("ssm_sb", [33, TS], F32)
    r_bc = sb1("r_bc", [128, TS], F32)
    b_bc = sb1("b_bc", [128, 2 * TS], F32)
    c_bc = sb1("c_bc", [128, 2 * TS], F32)
    mask_bc = c_bc
    a_buf = sb1("a_buf", [128, 2 * TS], F32)
    bx_buf = sb1("bx_buf", [128, 2 * TS], F32)
    h_buf = sb1("h_buf", [128, TS], F32)
    xcb_buf = sb1("xcb_buf", [128, TS], BF16)
    zpan = sb1("zpan", [128, NK * 128], BF16)

    tp_ps = [ps1(f"tp_ps{i}", [128, 1024], BF16) for i in range(2)]
    mm_ps = [ps1(f"mm_ps{i}", [128, 512], F32) for i in range(2)]
    z_ps = [ps1(f"z_ps{i}", [128, 512], F32) for i in range(2)]
    ssm_ps = ps1("ssm_ps", [33, TS], F32)
    pad_ps = ps1("pad_ps", [128, 512], F32)

    E.dma("g", wx_sb[:, :], wx_r.ap(), "dG")
    d_gsetup = E.dma("g", win_x[:, :].rearrange("p (k c) -> p k c", k=NK),
                     win.ap()[:, 0:D_INNER].rearrange("(k p) c -> p k c", p=128),
                     "dG")
    vv = E.op("v", lambda e: e.memset(xp_sb[:, :], 0.0), inc=("vS", 1))
    E.wait("t", "dG", d_gsetup)
    E.wait("t", "vS", vv)

    state = dict(mm_war=[None, None], z_war=[None, None], tp_war=[None, None],
                 abuf_war=[None, None], a2t_use_t=0, a2sb_use_t=0,
                 ygst_war=0, xsb_use=("vS", 0), ssm_bc0=0, ssm_bc1=0,
                 ssm_sd=0, wx_mm=0,
                 rbc_use=0, maskbc_use=0)

    def c_(i, k):
        return st[:, 14 * i + k:14 * i + k + 1]

    def load_stats_a2_transpose(own, k, xsrc, xoff):
        sev = None
        for i in range(4):
            E.wait("sy", *state["xsb_use"])
            d_x = E.dma("sy", x_sb[:, :],
                        xsrc.ap()[xoff + i * 128:xoff + (i + 1) * 128, :], "dS")
            E.wait("v", "dS", d_x)
            xt = x_sb[:, :]
            E.op("v", lambda e, xt=xt, i=i: e.reduce_sum(c_(i, 0), xt, axis=AX.X))
            E.op("v", lambda e, xt=xt: e.tensor_tensor(
                tmp_buf[:, :], xt[:, 0:512], xt[:, 0:512], op=ALU.mult))
            E.vstep(lambda e, i=i: e.reduce_sum(c_(i, 1), tmp_buf[:, :], axis=AX.X))
            E.op("v", lambda e, xt=xt: e.tensor_tensor(
                tmp_buf[:, :], xt[:, 512:1024], xt[:, 512:1024], op=ALU.mult))
            E.vstep(lambda e, i=i: e.reduce_sum(c_(i, 2), tmp_buf[:, :], axis=AX.X))
            E.vstep(lambda e, i=i: e.tensor_scalar(
                out=c_(i, 3), in0=c_(i, 0), scalar1=1.0 / D_MODEL, scalar2=None,
                op0=ALU.mult))
            E.vstep(lambda e, i=i: e.tensor_tensor(c_(i, 4), c_(i, 1), c_(i, 2),
                                                   op=ALU.add))
            E.vstep(lambda e, i=i: e.tensor_scalar(
                out=c_(i, 4), in0=c_(i, 4), scalar1=1.0 / D_MODEL, scalar2=None,
                op0=ALU.mult))
            E.vstep(lambda e, i=i: e.tensor_tensor(c_(i, 5), c_(i, 3), c_(i, 3),
                                                   op=ALU.mult))
            lv = E.vstep(lambda e, i=i: e.tensor_tensor(c_(i, 6), c_(i, 4), c_(i, 5),
                                                        op=ALU.subtract))
            E.wait("s", "vS", lv)
            E.sstep(lambda e, i=i: e.activation(c_(i, 7), c_(i, 6), AF.Ln,
                                                bias=eps_sb[:, 0:1]))
            sl = E.sstep(lambda e, i=i: e.activation(c_(i, 8), c_(i, 7), AF.Exp,
                                                     scale=eps_sb[:, 2:3]))
            E.wait("v", "sS", sl)
            E.vstep(lambda e, i=i: e.tensor_tensor(c_(i, 9), c_(i, 8), c_(i, 8),
                                                   op=ALU.mult))
            lv = E.vstep(lambda e, i=i: e.tensor_tensor(c_(i, 9), c_(i, 6), c_(i, 9),
                                                        op=ALU.mult))
            E.wait("s", "vS", lv)
            E.sstep(lambda e, i=i: e.activation(c_(i, 10), c_(i, 9), AF.Ln,
                                                bias=eps_sb[:, 1:2]))
            sl = E.sstep(lambda e, i=i: e.activation(c_(i, 11), c_(i, 10), AF.Exp,
                                                     scale=eps_sb[:, 2:3]))
            E.wait("v", "sS", sl)
            E.vstep(lambda e, i=i: e.tensor_tensor(c_(i, 12), c_(i, 8), c_(i, 11),
                                                   op=ALU.mult))
            lv = E.vstep(lambda e, i=i: e.tensor_tensor(c_(i, 13), c_(i, 3), c_(i, 12),
                                                        op=ALU.mult))
            lv = E.vstep(lambda e, i=i: e.tensor_scalar(
                out=c_(i, 13), in0=c_(i, 13), scalar1=-1.0, scalar2=None,
                op0=ALU.mult))
            if own:
                bt = k * 4 + i
                E.vstep(lambda e, i=i, bt=bt: e.tensor_copy(
                    s1_all[:, bt:bt + 1], c_(i, 8)))
                E.vstep(lambda e, i=i: e.tensor_tensor(
                    st[:, 60:61], c_(i, 3), c_(i, 8), op=ALU.mult))
                lv = E.vstep(lambda e, i=i, bt=bt: e.tensor_scalar(
                    out=b1_all[:, bt:bt + 1], in0=st[:, 60:61], scalar1=-1.0,
                    scalar2=None, op0=ALU.mult))
            E.wait("v", "tS", state["a2sb_use_t"])
            sl = E.op("v", lambda e, i=i: e.tensor_scalar(
                out=a2_sb[:, :], in0=x_sb[:, :], scalar1=c_(i, 12),
                scalar2=c_(i, 13), op0=ALU.mult, op1=ALU.add), inc=("vS", 1))
            state["xsb_use"] = ("vS", sl)
            E.wait("t", "vS", sl)
            for kk in range(NK):
                bank = kk % 2
                if state["tp_war"][bank] is not None:
                    E.wait("t", "sS", state["tp_war"][bank])
                tl = E.op("t", lambda e, kk=kk, bank=bank: e.transpose(
                    tp_ps[bank][:, 0:128], a2_sb[:, kk * 128:(kk + 1) * 128],
                    ident[:, :]), inc=("tS", 1))
                state["a2sb_use_t"] = tl
                E.wait("s", "tS", tl)
                E.wait("s", "tS", state["a2t_use_t"])
                sev = E.op("s", lambda e, kk=kk, i=i, bank=bank: e.activation(
                    a2t[:, kk * TS + i * 128:kk * TS + (i + 1) * 128],
                    tp_ps[bank][:, 0:128], AF.Copy,
                    scale=normw_sb[:, kk:kk + 1]), inc=("sS", 1))
                state["tp_war"][bank] = sev
        return sev

    def mamba_subchunk(k, own, xsrc, xoff, mask_off, skip_tail):
        if mask_off is not None:
            E.wait("sy", "vS", state["maskbc_use"])
            d_mask = E.dma("sy", mask_bc[:, 0:TS],
                           maskpre.ap()[0:1, mask_off:mask_off + TS]
                           .broadcast_to([128, TS]), "dS")
        a2t_ready = load_stats_a2_transpose(own, k, xsrc, xoff)

        if not skip_tail:
            for j in range(NJ):
                E.op("v", lambda e, j=j: e.tensor_copy(
                    tail_sb[:, j * 3:(j + 1) * 3],
                    xp_sb[:, j * FFW + TS:j * FFW + TS + 3]))
            E.vstep(lambda e: e.engine_nop())
            for j in range(NJ):
                E.op("v", lambda e, j=j: e.tensor_copy(
                    xp_sb[:, j * FFW:j * FFW + 3], tail_sb[:, j * 3:(j + 1) * 3]))
        v_tail = E.op("v", lambda e: e.engine_nop(), inc=("vS", 1))

        E.wait("t", "sS", a2t_ready)
        for j in range(NJ):
            war = state["mm_war"][j % 2]
            if war is not None:
                E.wait("t", "vS", war)
            mml = None
            for kk in range(NK):
                mml = E.op("t", lambda e, j=j, kk=kk: e.matmul(
                    mm_ps[j % 2][:, :],
                    win_x[:, kk * D_INNER + j * 128:kk * D_INNER + (j + 1) * 128],
                    a2t[:, kk * TS:(kk + 1) * TS],
                    start=(kk == 0), stop=(kk == NK - 1)),
                    inc=("tS", 1))
            state["a2t_use_t"] = mml
            E.wait("v", "tS", mml)
            if j == 0:
                E.wait("v", "vS", v_tail)
            ev = E.op("v", lambda e, j=j: e.tensor_copy(
                xp_sb[:, j * FFW + 3:j * FFW + 3 + TS], mm_ps[j % 2][:, :]),
                inc=("vS", 1))
            state["mm_war"][j % 2] = ev

        s_silu = None
        for j in range(NJ):
            xcj = xc_sb[:, j * TS:(j + 1) * TS]
            E.op("v", lambda e, j=j, xcj=xcj: e.tensor_scalar(
                out=xcj, in0=xp_sb[:, j * FFW:j * FFW + TS],
                scalar1=cw_sb[:, j * 4:j * 4 + 1], scalar2=None, op0=ALU.mult))
            for kk in range(1, 4):
                E.op("v", lambda e, j=j, kk=kk, xcj=xcj: e.scalar_tensor_tensor(
                    xcj, xp_sb[:, j * FFW + kk:j * FFW + kk + TS],
                    cw_sb[:, j * 4 + kk:j * 4 + kk + 1], xcj,
                    op0=ALU.mult, op1=ALU.add))
            cv = E.op("v", lambda e: e.engine_nop(), inc=("vS", 1))
            E.wait("s", "vS", cv)
            s_silu = E.op("s", lambda e, xcj=xcj, j=j: e.activation(
                xcj, xcj, AF.Silu, bias=cb_sb[:, j:j + 1]), inc=("sS", 1))
            if own:
                E.wait("v", "sS", s_silu)
                if j == 0:
                    E.wait("v", "dY", state["ygst_war"])
                E.op("v", lambda e, j=j, xcj=xcj: e.tensor_scalar(
                    out=y_sb[:, j * TS:(j + 1) * TS], in0=xcj,
                    scalar1=dD_sb[:, j:j + 1], scalar2=None, op0=ALU.mult))

        E.wait("v", "sS", s_silu)
        E.wait("v", "dG", state["ssm_sd"])
        mml = None
        for j in range(NJ):
            if j >= 1:
                E.wait("v", "tS", state["wx_mm"])
            cb = E.op("v", lambda e, j=j: e.tensor_copy(
                xcb_buf[:, :], xc_sb[:, j * TS:(j + 1) * TS]), inc=("vS", 1))
            E.wait("t", "vS", cb)
            mml = E.op("t", lambda e, j=j: e.matmul(
                ssm_ps[:, :], wx_sb[:, j * 33:(j + 1) * 33],
                xcb_buf[:, :],
                start=(j == 0), stop=(j == NJ - 1)), inc=("tS", 1))
            state["wx_mm"] = mml
        E.wait("v", "tS", mml)
        v_ssm = E.op("v", lambda e: e.tensor_copy(ssm_sb[:, :], ssm_ps[:, :]),
                     inc=("vS", 1))
        E.wait("t", "vS", v_ssm)
        E.wait("g", "vS", v_ssm)
        E.wait("g", "dB0", state["ssm_bc0"])
        E.wait("g", "dB1", state["ssm_bc1"])
        d_sd = E.dma("g", ssm_dram.ap(), ssm_sb[:, :], "dG")
        state["ssm_sd"] = d_sd
        E.wait("g", "sS", state["rbc_use"])
        d_r = E.dma("g", r_bc[:, :],
                    ssm_dram.ap()[0:1, :].broadcast_to([128, TS]), "dG")

        E.wait("s", "dG", d_r)
        v_dt = None
        for j in range(NJ):
            dtj = dt_sb[:, j * TS:(j + 1) * TS]
            E.op("s", lambda e, j=j, dtj=dtj: e.activation(
                dtj, r_bc[:, :], AF.Relu, scale=wdt_sb[:, j:j + 1],
                bias=bdt_sb[:, j:j + 1]))
            E.op("s", lambda e, j=j: e.activation(
                tmp_buf[:, :], r_bc[:, :], AF.Abs, scale=wdt_sb[:, j:j + 1],
                bias=bdt_sb[:, j:j + 1]))
            E.op("s", lambda e: e.activation(tmp_buf[:, :], tmp_buf[:, :], AF.Exp,
                                             scale=-1.0))
            sl = E.op("s", lambda e: e.activation(tmp_buf[:, :], tmp_buf[:, :], AF.Ln,
                                                  bias=1.0), inc=("sS", 1))
            E.wait("v", "sS", sl)
            if mask_off is not None and j == 0:
                E.wait("v", "dS", d_mask)
            v_dt = E.op("v", lambda e, dtj=dtj: e.tensor_tensor(
                dtj, dtj, tmp_buf[:, :], op=ALU.add), inc=("vS", 1))
            if mask_off is not None:
                v_dt = E.op("v", lambda e, dtj=dtj: e.tensor_tensor(
                    dtj, dtj, mask_bc[:, 0:TS], op=ALU.mult), inc=("vS", 1))
            else:
                v_dt = E.op("v", lambda e, dtj=dtj: e.tensor_scalar(
                    out=dtj, in0=dtj, scalar1=-1.0, scalar2=None, op0=ALU.mult),
                    inc=("vS", 1))
            E.wait("s", "vS", v_dt)
        state["rbc_use"] = E.cnt["sS"]
        state["maskbc_use"] = v_dt if mask_off is not None else state["maskbc_use"]

        v_u = None
        for j in range(NJ):
            v_u = E.op("v", lambda e, j=j: e.tensor_tensor(
                xc_sb[:, j * TS:(j + 1) * TS], xc_sb[:, j * TS:(j + 1) * TS],
                dt_sb[:, j * TS:(j + 1) * TS], op=ALU.mult), inc=("vS", 1))

        s_gz = {}
        if own:
            for j in range(NJ):
                E.op("v", lambda e, j=j: e.tensor_copy(
                    tail_sb[:, j * 3:(j + 1) * 3],
                    xp_sb[:, j * FFW + TS:j * FFW + TS + 3]))
            v_tail2 = E.op("v", lambda e: e.engine_nop(), inc=("vS", 1))
            zp_use = 0
            for m in range(NJ):
                E.wait("g", "tS", zp_use)
                d_zp = E.dma(
                    "g", zpan[:, :].rearrange("p (k c) -> p k c", k=NK),
                    win.ap()[:, D_INNER + m * 128:D_INNER + (m + 1) * 128]
                    .rearrange("(k p) c -> p k c", p=128), "dG")
                war = state["z_war"][m % 2]
                if war is not None:
                    E.wait("t", "sS", war)
                E.wait("t", "dG", d_zp)
                mml = None
                for kk in range(NK):
                    mml = E.op("t", lambda e, m=m, kk=kk: e.matmul(
                        z_ps[m % 2][:, :],
                        zpan[:, kk * 128:(kk + 1) * 128],
                        a2t[:, kk * TS:(kk + 1) * TS],
                        start=(kk == 0), stop=(kk == NK - 1)), inc=("tS", 1))
                zp_use = mml
                state["a2t_use_t"] = mml
                E.wait("s", "tS", mml)
                if m == 0:
                    E.wait("s", "vS", v_tail2)
                sg = E.op("s", lambda e, m=m: e.activation(
                    xp_sb[:, m * FFW:m * FFW + TS], z_ps[m % 2][:, :], AF.Silu),
                    inc=("sS", 1))
                state["z_war"][m % 2] = sg
                s_gz[m] = sg

        E.wait("s", "vS", v_u)
        for sig in range(16):
            par = sig % 2
            E.wait("g", "vS", state.get(f"bbc_use{par}", 0))
            d_b = E.dma("g", b_bc[:, par * TS:(par + 1) * TS],
                        ssm_dram.ap()[1 + sig:2 + sig, :].broadcast_to([128, TS]),
                        f"dB{par}")
            if own:
                d_c = E.dma("g", c_bc[:, par * TS:(par + 1) * TS],
                            ssm_dram.ap()[17 + sig:18 + sig, :]
                            .broadcast_to([128, TS]), f"dB{par}")
            for j in range(NJ):
                col = j * 16 + sig
                pb = (sig * NJ + j) % 2
                war = state["abuf_war"][pb]
                if war is not None:
                    E.wait("s", "vS", war)
                sa = E.op("s", lambda e, j=j, col=col, pb=pb: e.activation(
                    a_buf[:, pb * TS:(pb + 1) * TS], dt_sb[:, j * TS:(j + 1) * TS],
                    AF.Exp, scale=a_sb[:, col:col + 1]), inc=("sS", 1))
                E.wait("v", "sS", sa)
                if j == 0:
                    E.wait("v", f"dB{par}", d_c if own else d_b)
                E.op("v", lambda e, j=j, par=par, pb=pb: e.tensor_tensor(
                    bx_buf[:, pb * TS:(pb + 1) * TS],
                    xc_sb[:, j * TS:(j + 1) * TS],
                    b_bc[:, par * TS:(par + 1) * TS], op=ALU.mult))
                sc = E.op("v", lambda e, col=col, pb=pb: e.tensor_tensor_scan(
                    h_buf[:, :], a_buf[:, pb * TS:(pb + 1) * TS],
                    bx_buf[:, pb * TS:(pb + 1) * TS],
                    carry[:, col:col + 1], op0=ALU.mult, op1=ALU.add),
                    inc=("vS", 1))
                state["abuf_war"][pb] = sc
                E.wait("v", "vS", sc)
                if dbg and own and sig == 0 and j == 0 and (xoff - 8) // TS == 0:
                    E.wait("s", "vS", sc)
                    E.dma("s", dbg_t["dbg_asb"].ap(), a_sb[:, :], "dA")
                    E.dma("s", dbg_t["dbg_abxh"].ap()[:, 0:TS],
                          a_buf[:, pb * TS:(pb + 1) * TS], "dA")
                    E.dma("s", dbg_t["dbg_abxh"].ap()[:, TS:2 * TS], bx_buf[:, :],
                          "dA")
                    E.dma("s", dbg_t["dbg_abxh"].ap()[:, 2 * TS:3 * TS], h_buf[:, :],
                          "dA")
                    dbgd = E.cnt["dA"]
                    E.wait("v", "dA", dbgd)
                E.op("v", lambda e, col=col: e.tensor_copy(
                    carry[:, col:col + 1], h_buf[:, TS - 1:TS]))
                if own:
                    E.op("v", lambda e, par=par: e.tensor_tensor(
                        tmp_buf[:, :], h_buf[:, :],
                        c_bc[:, par * TS:(par + 1) * TS], op=ALU.mult))
                    E.op("v", lambda e, j=j: e.tensor_tensor(
                        y_sb[:, j * TS:(j + 1) * TS], y_sb[:, j * TS:(j + 1) * TS],
                        tmp_buf[:, :], op=ALU.subtract))
            lastv = E.op("v", lambda e: e.engine_nop(), inc=("vS", 1))
            state[f"bbc_use{par}"] = lastv
        state["ssm_bc0"] = E.cnt["dB0"]
        state["ssm_bc1"] = E.cnt["dB1"]

        if own:
            ksub = (xoff - 8) // TS
            if dbg and ksub == 0:
                E.wait("s", "vS", E.cnt["vS"])
                E.wait("s", "sS", E.cnt["sS"])
                for nm, sbuf in (("dbg_xp", xp_sb), ("dbg_xc", xc_sb),
                                 ("dbg_dt", dt_sb), ("dbg_carry", carry),
                                 ("dbg_ssm", ssm_sb), ("dbg_a2t", a2t),
                                 ("dbg_y", y_sb)):
                    E.dma("s", dbg_t[nm].ap(), sbuf[:, :], "dA")
                dbg_done = E.cnt["dA"]
                for qn in ("v", "g", "sy", "t"):
                    E.wait(qn, "dA", dbg_done)
                E.wait("s", "dA", dbg_done)
            for j in range(NJ):
                E.wait("v", "sS", s_gz[j])
                gv = E.op("v", lambda e, j=j: e.tensor_tensor(
                    y_sb[:, j * TS:(j + 1) * TS], y_sb[:, j * TS:(j + 1) * TS],
                    xp_sb[:, j * FFW:j * FFW + TS], op=ALU.mult), inc=("vS", 1))
                E.wait("g", "vS", gv)
                E.dma("g", ygb_scr.ap()[j, :, ksub * TS:(ksub + 1) * TS],
                      y_sb[:, j * TS:(j + 1) * TS], "dY")
            state["ygst_war"] = E.cnt["dY"]

    for k in range(NSUB_PRE):
        mamba_subchunk(k, own=False, xsrc=xpre, xoff=k * TS, mask_off=k * TS,
                       skip_tail=(k == 0))


    E.wait("sy", *state["xsb_use"])
    d_pad = E.dma("sy", x_sb[0:8, 0:D_MODEL], xown.ap()[0:8, :], "dS")
    E.wait("v", "dS", d_pad)
    pc = lambda k: st[0:8, k:k + 1]
    E.op("v", lambda e: e.reduce_sum(pc(0), x_sb[0:8, 0:D_MODEL], axis=AX.X))
    E.op("v", lambda e: e.tensor_tensor(tmp_buf[0:8, :], x_sb[0:8, 0:512],
                                        x_sb[0:8, 0:512], op=ALU.mult))
    E.vstep(lambda e: e.reduce_sum(pc(1), tmp_buf[0:8, :], axis=AX.X))
    E.op("v", lambda e: e.tensor_tensor(tmp_buf[0:8, :], x_sb[0:8, 512:1024],
                                        x_sb[0:8, 512:1024], op=ALU.mult))
    E.vstep(lambda e: e.reduce_sum(pc(2), tmp_buf[0:8, :], axis=AX.X))
    E.vstep(lambda e: e.tensor_scalar(out=pc(3), in0=pc(0), scalar1=1.0 / D_MODEL,
                                      scalar2=None, op0=ALU.mult))
    E.vstep(lambda e: e.tensor_tensor(pc(4), pc(1), pc(2), op=ALU.add))
    E.vstep(lambda e: e.tensor_scalar(out=pc(4), in0=pc(4), scalar1=1.0 / D_MODEL,
                                      scalar2=None, op0=ALU.mult))
    E.vstep(lambda e: e.tensor_tensor(pc(5), pc(3), pc(3), op=ALU.mult))
    pv = E.vstep(lambda e: e.tensor_tensor(pc(6), pc(4), pc(5), op=ALU.subtract))
    E.wait("s", "vS", pv)
    E.sstep(lambda e: e.activation(pc(7), pc(6), AF.Ln, bias=eps_sb[0:8, 0:1]))
    sx = E.sstep(lambda e: e.activation(pc(8), pc(7), AF.Exp,
                                        scale=eps_sb[0:8, 2:3]))
    E.wait("v", "sS", sx)
    E.vstep(lambda e: e.tensor_tensor(pc(9), pc(8), pc(8), op=ALU.mult))
    pv = E.vstep(lambda e: e.tensor_tensor(pc(9), pc(6), pc(9), op=ALU.mult))
    E.wait("s", "vS", pv)
    E.sstep(lambda e: e.activation(pc(10), pc(9), AF.Ln, bias=eps_sb[0:8, 1:2]))
    sx = E.sstep(lambda e: e.activation(pc(11), pc(10), AF.Exp,
                                        scale=eps_sb[0:8, 2:3]))
    E.wait("v", "sS", sx)
    E.vstep(lambda e: e.tensor_tensor(pc(12), pc(8), pc(11), op=ALU.mult))
    E.vstep(lambda e: e.tensor_tensor(pc(13), pc(3), pc(12), op=ALU.mult))
    pv = E.vstep(lambda e: e.tensor_scalar(out=pc(13), in0=pc(13), scalar1=-1.0,
                                           scalar2=None, op0=ALU.mult))
    E.wait("v", "tS", state["a2sb_use_t"])
    sx = E.op("v", lambda e: e.tensor_scalar(
        out=a2_sb[0:8, 0:D_MODEL], in0=x_sb[0:8, 0:D_MODEL], scalar1=pc(12),
        scalar2=pc(13), op0=ALU.mult, op1=ALU.add), inc=("vS", 1))
    state["xsb_use"] = ("vS", sx)
    E.wait("t", "vS", sx)
    tl = None
    for kk in range(NK):
        war = state["tp_war"][0]
        if war is not None and kk == 0:
            E.wait("t", "sS", war)
        tl = E.op("t", lambda e, kk=kk: e.transpose(
            tp_ps[0][:, kk * 8:(kk + 1) * 8], a2_sb[0:8, kk * 128:(kk + 1) * 128],
            ident[0:8, 0:8]), inc=("tS", 1))
    state["a2sb_use_t"] = tl
    E.wait("s", "tS", tl)
    E.wait("s", "tS", state["a2t_use_t"])
    sx = None
    for kk in range(NK):
        sx = E.op("s", lambda e, kk=kk: e.activation(
            a2t[:, kk * TS:kk * TS + 8], tp_ps[0][:, kk * 8:(kk + 1) * 8],
            AF.Copy, scale=normw_sb[:, kk:kk + 1]), inc=("sS", 1))
    state["tp_war"][0] = sx
    E.wait("t", "sS", sx)
    pvl = None
    for j in range(NJ):
        if pvl is not None:
            E.wait("t", "vS", pvl)
        mml = None
        for kk in range(NK):
            mml = E.op("t", lambda e, j=j, kk=kk: e.matmul(
                pad_ps[:, 0:8],
                win_x[:, kk * D_INNER + j * 128:kk * D_INNER + (j + 1) * 128],
                a2t[:, kk * TS:kk * TS + 8],
                start=(kk == 0), stop=(kk == NK - 1)), inc=("tS", 1))
        E.wait("v", "tS", mml)
        pvl = E.op("v", lambda e, j=j: e.tensor_copy(
            xp_sb[:, j * FFW:j * FFW + 3], pad_ps[:, 5:8]), inc=("vS", 1))
    state["a2t_use_t"] = E.cnt["tS"]

    for k in range(NSUB_OWN):
        mamba_subchunk(k, own=True, xsrc=xown, xoff=8 + k * TS, mask_off=None,
                       skip_tail=(k == 0))


    E.wait("t", "vS", E.cnt["vS"])
    E.wait("t", "sS", E.cnt["sS"])
    E.wait("t", "dA", E.cnt["dA"])
    fence_t = E.op("t", lambda e: e.nop(), inc=("tS", 1))
    for qn in ("v", "s", "g", "sy"):
        E.wait(qn, "tS", fence_t)
    es1.close()

    es2 = ExitStack()
    sb2 = lambda n, sh, dt=F32: es2.enter_context(nc.sbuf_tensor(n, sh, dt))
    ps2 = lambda n, sh, dt=F32: es2.enter_context(nc.psum_tensor(n, sh, dt))

    x2_sb = sb2("x2_sb", [128, 4 * D_MODEL], F32)
    a22_sb = sb2("a22_sb", [128, 4 * D_MODEL], BF16)
    a1t = sb2("a1t", [128, NK * T_OWN], BF16)
    gbuf = sb2("gbuf", [128, 2 * TS], F32)
    hbst = sb2("hbst", [128, 2 * 8 * TS], BF16)
    w1p = sb2("w1p", [128, 2 * NK * 128], BF16)
    v1p = sb2("v1p", [128, 2 * NK * 128], BF16)
    tp2_ps = [ps2(f"tp2_ps{i}", [128, 1024], BF16) for i in range(2)]
    h_ps = [ps2(f"h_ps{i}", [128, TS], F32) for i in range(4)]
    r_ps = ps2("r_ps", [128, 512], F32)

    tp2_war = [None, None]
    x2_use = 0
    for k in range(NSUB_OWN):
        E.wait("sy", "vS", x2_use if x2_use else 0)
        d_x = E.dma("sy", x2_sb[:, :].rearrange("p (i d) -> p i d", i=4),
                    xown.ap()[8 + k * TS:8 + (k + 1) * TS, :]
                    .rearrange("(i p) d -> p i d", p=128), "dS")
        E.wait("v", "dS", d_x)
        sa = None
        for i in range(4):
            tt = k * 4 + i
            sa = E.op("v", lambda e, i=i, tt=tt: e.tensor_scalar(
                out=a22_sb[:, i * D_MODEL:(i + 1) * D_MODEL],
                in0=x2_sb[:, i * D_MODEL:(i + 1) * D_MODEL],
                scalar1=s1_all[:, tt:tt + 1], scalar2=b1_all[:, tt:tt + 1],
                op0=ALU.mult, op1=ALU.add), inc=("vS", 1))
        x2_use = sa
        E.wait("t", "vS", sa)
        for kk in range(NK):
            if tp2_war[kk % 2] is not None:
                E.wait("t", "sS", tp2_war[kk % 2])
            tl = None
            for i in range(4):
                tl = E.op("t", lambda e, kk=kk, i=i: e.transpose(
                    tp2_ps[kk % 2][:, i * 128:(i + 1) * 128],
                    a22_sb[:, i * D_MODEL + kk * 128:i * D_MODEL + (kk + 1) * 128],
                    ident[:, :]), inc=("tS", 1))
            E.wait("s", "tS", tl)
            sev = E.op("s", lambda e, kk=kk, k=k: e.activation(
                a1t[:, kk * T_OWN + k * TS:kk * T_OWN + (k + 1) * TS],
                tp2_ps[kk % 2][:, 0:512], AF.Copy), inc=("sS", 1))
            tp2_war[kk % 2] = sev
    a1t_ready = E.cnt["sS"]

    E.wait("t", "sS", a1t_ready)
    rp_use = 0
    for g4 in range(NT_OWN // 4):
        E.wait("t", "vS", rp_use)
        mml = None
        for i in range(4):
            tt = g4 * 4 + i
            for kk in range(NK):
                mml = E.op("t", lambda e, tt=tt, kk=kk, i=i: e.matmul(
                    r_ps[:, i * NE:(i + 1) * NE],
                    a1t[:, kk * T_OWN + tt * 128:kk * T_OWN + (tt + 1) * 128],
                    rw_sb[:, kk * NE:(kk + 1) * NE],
                    start=(kk == 0), stop=(kk == NK - 1)), inc=("tS", 1))
        E.wait("v", "tS", mml)
        rp_use = E.op("v", lambda e, g4=g4: e.tensor_copy(
            sc_sb[:, g4 * 4 * NE:(g4 + 1) * 4 * NE], r_ps[:, 0:4 * NE]),
            inc=("vS", 1))
    for tt in range(NT_OWN):
        p = sc_sb[:, tt * NE:(tt + 1) * NE]
        mx = st[:, 0:1]
        E.vstep(lambda e, p=p, mx=mx: e.reduce_max(mx, p, axis=AX.X))
        E.vstep(lambda e, p=p, mx=mx: e.tensor_scalar(
            out=p, in0=p, scalar1=mx, scalar2=None, op0=ALU.subtract))
    E.wait("s", "vS", E.cnt["vS"])
    sxx = E.op("s", lambda e: e.activation(sc_sb[:, 0:NT_OWN * NE],
                                           sc_sb[:, 0:NT_OWN * NE], AF.Exp),
               inc=("sS", 1))
    E.wait("s", "sS", sxx)
    E.wait("v", "sS", sxx)
    for tt in range(NT_OWN):
        p = sc_sb[:, tt * NE:(tt + 1) * NE]
        m = m_sb[:, tt * NE:(tt + 1) * NE]
        sm = st[:, 1:2]
        m1 = st[:, 2:3]
        m2 = st[:, 3:4]
        E.vstep(lambda e, p=p, sm=sm: e.reduce_sum(sm, p, axis=AX.X))
        E.vstep(lambda e, sm=sm: e.reciprocal(sm, sm))
        E.vstep(lambda e, p=p, sm=sm: e.tensor_scalar(
            out=p, in0=p, scalar1=sm, scalar2=None, op0=ALU.mult))
        E.vstep(lambda e, p=p, m1=m1: e.reduce_max(m1, p, axis=AX.X))
        E.vstep(lambda e, p=p, m=m, m1=m1: e.tensor_scalar(
            out=m, in0=p, scalar1=m1, scalar2=1e30, op0=ALU.is_ge, op1=ALU.mult))
        E.vstep(lambda e, p=p, m=m: e.tensor_tensor(m, p, m, op=ALU.subtract))
        E.vstep(lambda e, m=m, m2=m2: e.reduce_max(m2, m, axis=AX.X))
        E.vstep(lambda e, p=p, m=m, m2=m2: e.tensor_scalar(
            out=m, in0=p, scalar1=m2, scalar2=None, op0=ALU.is_ge))
        E.vstep(lambda e, p=p, m=m: e.tensor_tensor(m, p, m, op=ALU.mult))

    if dbg:
        E.wait("s", "vS", E.cnt["vS"])
        E.dma("s", dbg_t["dbg_m"].ap()[:, 0:NT_OWN * NE], m_sb[:, :], "dA")
        E.dma("s", dbg_t["dbg_sc"].ap()[:, 0:NT_OWN * NE], sc_sb[:, :], "dA")
        E.dma("s", dbg_t["dbg_s1"].ap()[:, 0:NT_OWN], s1_all[:, :], "dA")
        E.dma("s", dbg_t["dbg_b1"].ap()[:, 0:NT_OWN], b1_all[:, :], "dA")
        E.dma("s", dbg_t["dbg_a1t"].ap(), a1t[:, 0:NK * TS], "dA")
        dbg_done = E.cnt["dA"]
        for qn in ("v", "g", "sy", "t"):
            E.wait(qn, "dA", dbg_done)
        E.wait("s", "dA", dbg_done)
    E.wait("t", "vS", E.cnt["vS"])
    panel_use = {}
    hps_war_v = [None, None]
    hps_war_s = [None, None]
    hbst_dma = [0, 0]
    for ee in range(NE):
        for fb in range(NF):
            it = ee * NF + fb
            pp = it % 2
            E.wait("g", "tS", panel_use.get(pp, 0))
            E.dma("g", w1p[:, pp * NK * 128:(pp + 1) * NK * 128]
                  .rearrange("p (k f) -> p k f", k=NK),
                  w1.ap()[ee, :, fb * 128:(fb + 1) * 128]
                  .rearrange("(k p) f -> p k f", p=128), "dG")
            d_w1v = E.cnt["dG"]
            d_v1 = E.dma("g", v1p[:, pp * NK * 128:(pp + 1) * NK * 128]
                         .rearrange("p (k f) -> p k f", k=NK),
                         v1.ap()[ee, :, fb * 128:(fb + 1) * 128]
                         .rearrange("(k p) f -> p k f", p=128), "dV")
            E.wait("t", "dG", d_w1v)
            E.wait("t", "dV", d_v1)
            hoff = (it % 2) * 8 * TS
            E.wait("v", f"dH{it % 2}", hbst_dma[it % 2])
            mml = None
            for tsb in range(T_OWN // TS):
                pb = tsb % 2
                if hps_war_s[pb] is not None:
                    E.wait("t", "sS", hps_war_s[pb])
                    E.wait("t", "vS", hps_war_v[pb])
                for kk in range(NK):
                    E.op("t", lambda e, pp=pp, kk=kk, tsb=tsb, pb=pb: e.matmul(
                        h_ps[2 * pb][:, :],
                        w1p[:, pp * NK * 128 + kk * 128:pp * NK * 128 + (kk + 1) * 128],
                        a1t[:, kk * T_OWN + tsb * TS:kk * T_OWN + (tsb + 1) * TS],
                        start=(kk == 0), stop=(kk == NK - 1)),
                        inc=("tS", 1) if kk == NK - 1 else None)
                m1l = E.cnt["tS"]
                for kk in range(NK):
                    E.op("t", lambda e, pp=pp, kk=kk, tsb=tsb, pb=pb: e.matmul(
                        h_ps[2 * pb + 1][:, :],
                        v1p[:, pp * NK * 128 + kk * 128:pp * NK * 128 + (kk + 1) * 128],
                        a1t[:, kk * T_OWN + tsb * TS:kk * T_OWN + (tsb + 1) * TS],
                        start=(kk == 0), stop=(kk == NK - 1)),
                        inc=("tS", 1) if kk == NK - 1 else None)
                m2l = E.cnt["tS"]
                mml = m2l
                E.wait("s", "tS", m1l)
                if hps_war_v[pb] is not None:
                    E.wait("s", "vS", hps_war_v[pb])
                sg = E.op("s", lambda e, pb=pb: e.activation(
                    gbuf[:, pb * TS:(pb + 1) * TS], h_ps[2 * pb][:, :], AF.Gelu),
                    inc=("sS", 1))
                hps_war_s[pb] = sg
                E.wait("v", "tS", m2l)
                E.wait("v", "sS", sg)
                hv = E.op("v", lambda e, pb=pb, tsb=tsb, hoff=hoff: e.tensor_tensor(
                    hbst[:, hoff + tsb * TS:hoff + (tsb + 1) * TS],
                    gbuf[:, pb * TS:(pb + 1) * TS], h_ps[2 * pb + 1][:, :],
                    op=ALU.mult), inc=("vS", 1))
                hps_war_v[pb] = hv
            panel_use[pp] = mml
            E.wait("s", "vS", E.cnt["vS"])
            hbst_dma[it % 2] = E.dma("s", hb_scr.ap()[ee, fb, :, :],
                                     hbst[:, hoff:hoff + T_OWN],
                                     f"dH{it % 2}")

    E.wait("t", "vS", E.cnt["vS"])
    E.wait("t", "sS", E.cnt["sS"])
    E.wait("t", "dA", E.cnt["dA"])
    fence_t2 = E.op("t", lambda e: e.nop(), inc=("tS", 1))
    for qn in ("v", "s", "g", "sy"):
        E.wait(qn, "tS", fence_t2)
    es2.close()

    es3 = ExitStack()
    sb3 = lambda n, sh, dt=F32: es3.enter_context(nc.sbuf_tensor(n, sh, dt))
    ps3 = lambda n, sh, dt=F32: es3.enter_context(nc.psum_tensor(n, sh, dt))

    out_acc = sb3("out_acc", [128, NT_OWN * D_MODEL], F32)
    bias_bc = sb3("bias_bc", [128, D_MODEL], F32)
    s1p_all = sb3("s1p_all", [128, NT_OWN], F32)
    o_ps = [ps3(f"o_ps{i}", [128, 512], F32) for i in range(4)]

    es3a = ExitStack()
    x3_sb = es3a.enter_context(nc.sbuf_tensor("x3_sb", [128, 4 * D_MODEL], F32))
    ygt = es3a.enter_context(nc.sbuf_tensor("ygt", [128, NJ * TS], BF16))
    wout_sb = es3a.enter_context(nc.sbuf_tensor("wout_sb", [128, NJ * D_MODEL], BF16))

    E.vstep(lambda e: e.tensor_scalar(out=s1p_all[:, :], in0=s1_all[:, :],
                                      scalar1=1.0, scalar2=None, op0=ALU.add))
    d_wo = E.dma("g", wout_sb[:, :].rearrange("p (k c) -> p k c", k=NJ),
                 wout.ap().rearrange("(k p) c -> p k c", p=128), "dG")
    d_bb = E.dma("g", bias_bc[:, :],
                 fbias.ap()[0:1, :].broadcast_to([128, D_MODEL]), "dG")
    E.wait("t", "dG", d_wo)
    ygt_use = 0
    ops_use = [0, 0, 0, 0]
    for qd in range(T_OWN // TS):
        E.wait("sy", "vS", E.cnt["vS"])
        d_x = E.dma("sy", x3_sb[:, :].rearrange("p (i d) -> p i d", i=4),
                    xown.ap()[8 + qd * TS:8 + (qd + 1) * TS, :]
                    .rearrange("(i p) d -> p i d", p=128), "dS")
        dy = E.dma("sy", ygt[:, :].rearrange("p (j t) -> p j t", j=NJ),
                   ygb_scr.ap()[:, :, qd * TS:(qd + 1) * TS]
                   .rearrange("j p t -> p j t"), "dS")
        E.wait("v", "dS", d_x)
        E.wait("v", "dG", d_bb)
        for i in range(4):
            tt = qd * 4 + i
            E.op("v", lambda e, i=i, tt=tt: e.tensor_scalar(
                out=out_acc[:, tt * D_MODEL:(tt + 1) * D_MODEL],
                in0=x3_sb[:, i * D_MODEL:(i + 1) * D_MODEL],
                scalar1=s1p_all[:, tt:tt + 1], scalar2=b1_all[:, tt:tt + 1],
                op0=ALU.mult, op1=ALU.add))
            E.op("v", lambda e, i=i, tt=tt: e.tensor_tensor(
                out_acc[:, tt * D_MODEL:(tt + 1) * D_MODEL],
                out_acc[:, tt * D_MODEL:(tt + 1) * D_MODEL], bias_bc[:, :],
                op=ALU.add))
        vinit = E.op("v", lambda e: e.engine_nop(), inc=("vS", 1))
        E.wait("t", "dS", dy)
        E.wait("t", "vS", vinit)
        for i in range(4):
            tt = qd * 4 + i
            for half in range(2):
                pidx = 2 * (i % 2) + half
                E.wait("t", "vS", ops_use[pidx])
                mml = None
                for j in range(NJ):
                    mml = E.op("t", lambda e, j=j, i=i, half=half, pidx=pidx: e.matmul(
                        o_ps[pidx][:, :],
                        ygt[:, j * TS + i * 128:j * TS + (i + 1) * 128],
                        wout_sb[:, j * D_MODEL + half * 512:
                                j * D_MODEL + (half + 1) * 512],
                        start=(j == 0), stop=(j == NJ - 1)), inc=("tS", 1))
                E.wait("v", "tS", mml)
                vv = E.op("v", lambda e, tt=tt, half=half, pidx=pidx: e.tensor_tensor(
                    out_acc[:, tt * D_MODEL + half * 512:
                            tt * D_MODEL + (half + 1) * 512],
                    out_acc[:, tt * D_MODEL + half * 512:
                            tt * D_MODEL + (half + 1) * 512],
                    o_ps[pidx][:, :], op=ALU.add), inc=("vS", 1))
                ops_use[pidx] = vv
    if dbg:
        E.wait("s", "vS", E.cnt["vS"])
        E.dma("s", dbg_t["dbg_oi"].ap(), out_acc[:, 0:4 * D_MODEL], "dA")
        dbg_done = E.cnt["dA"]
        for qn in ("v", "g", "sy", "t"):
            E.wait(qn, "dA", dbg_done)
        E.wait("s", "dA", dbg_done)
    E.wait("t", "vS", E.cnt["vS"])
    fence_t3 = E.op("t", lambda e: e.nop(), inc=("tS", 1))
    for qn in ("v", "s", "g", "sy"):
        E.wait(qn, "tS", fence_t3)
    es3a.close()

    w2_sb = sb3("w2_sb", [128, NF * D_MODEL], BF16)
    hbq = sb3("hbq", [128, 2 * NF * 256], BF16)
    w2_use = 0
    hbq_use = [0, 0]
    for ee in range(NE):
        E.wait("g", "tS", w2_use)
        d_w2 = E.dma("g", w2_sb[:, :].rearrange("p (f c) -> p f c", f=NF),
                     w2.ap()[ee, :, :].rearrange("(f p) c -> p f c", p=128), "dG")
        E.wait("t", "dG", d_w2)
        for ch in range(T_OWN // 256):
            pb = ch % 2
            E.wait("sy", "tS", hbq_use[pb])
            d_hb = E.dma("sy",
                         hbq[:, pb * NF * 256:(pb + 1) * NF * 256]
                         .rearrange("p (f t) -> p f t", f=NF),
                         hb_scr.ap()[ee, :, :, ch * 256:(ch + 1) * 256]
                         .rearrange("f p t -> p f t"), "dS")
            E.wait("t", "dS", d_hb)
            for i2 in range(2):
                tt = ch * 2 + i2
                for half in range(2):
                    pidx = 2 * i2 + half
                    E.wait("t", "vS", ops_use[pidx])
                    mml = None
                    for fb in range(NF):
                        mml = E.op("t", lambda e, fb=fb, i2=i2, half=half, pb=pb,
                                   pidx=pidx: e.matmul(
                            o_ps[pidx][:, :],
                            hbq[:, pb * NF * 256 + fb * 256 + i2 * 128:
                                pb * NF * 256 + fb * 256 + (i2 + 1) * 128],
                            w2_sb[:, fb * D_MODEL + half * 512:
                                  fb * D_MODEL + (half + 1) * 512],
                            start=(fb == 0), stop=(fb == NF - 1)), inc=("tS", 1))
                    E.wait("v", "tS", mml)
                    vv = E.op("v", lambda e, tt=tt, half=half, ee=ee, pidx=pidx:
                              e.scalar_tensor_tensor(
                        out_acc[:, tt * D_MODEL + half * 512:
                                tt * D_MODEL + (half + 1) * 512],
                        o_ps[pidx][:, :],
                        m_sb[:, tt * NE + ee:tt * NE + ee + 1],
                        out_acc[:, tt * D_MODEL + half * 512:
                                tt * D_MODEL + (half + 1) * 512],
                        op0=ALU.mult, op1=ALU.add), inc=("vS", 1))
                    ops_use[pidx] = vv
            hbq_use[pb] = E.cnt["tS"]
        w2_use = E.cnt["tS"]

    E.wait("s", "vS", E.cnt["vS"])
    for tt in range(NT_OWN):
        E.dma("s", out_t.ap()[tt * 128:(tt + 1) * 128, :],
              out_acc[:, tt * D_MODEL:(tt + 1) * D_MODEL], "dA")
    E.wait("s", "dA", E.cnt["dA"])
    E.wait("g", "dA", E.cnt["dA"])
    E.wait("sy", "dS", E.cnt["dS"])
    E.wait("sy", "dA", E.cnt["dA"])

    with nc.Block() as block:
        @block.tensor
        def _(eng):
            E.replay(eng, "t")

        @block.vector
        def _(eng):
            E.replay(eng, "v")

        @block.scalar
        def _(eng):
            E.replay(eng, "s")

        @block.gpsimd
        def _(eng):
            E.replay(eng, "g")

        @block.sync
        def _(eng):
            E.replay(eng, "sy")

    es3.close()
    es0.close()
    return nc


def _tile_rows(a, nt):
    a2 = np.asarray(a, dtype=np.float32).reshape(nt, 128, -1)
    return np.ascontiguousarray(a2.transpose(1, 0, 2).reshape(128, -1))


def stage_inputs(cfg, x, norm_w, W_in, conv_w, conv_b, W_x, w_dt, b_dt, A_log,
                 D_param, W_out, router_w, w1, v1, w2, bias):
    T_PRE, T_OWN = cfg["T_PRE"], cfg["T_OWN"]
    x = np.asarray(x, dtype=np.float32)
    B, T, D = x.shape
    f32 = lambda a: np.ascontiguousarray(np.asarray(a, dtype=np.float32))
    base = {
        "win": f32(W_in), "wout": f32(W_out),
        "wx_r": _tile_rows(W_x, NJ), "alog_r": _tile_rows(A_log, NJ),
        "wdt_r": _tile_rows(np.reshape(w_dt, (-1, 1)), NJ),
        "bdt_r": _tile_rows(np.reshape(b_dt, (-1, 1)), NJ),
        "cw_r": _tile_rows(np.reshape(conv_w, (D_INNER, 4)), NJ),
        "cb_r": _tile_rows(np.reshape(conv_b, (-1, 1)), NJ),
        "dD_r": _tile_rows(np.reshape(D_param, (-1, 1)), NJ),
        "normw_r": _tile_rows(np.reshape(norm_w, (-1, 1)), NK),
        "rw_r": _tile_rows(router_w, NK),
        "w1": f32(w1), "v1": f32(v1), "w2": f32(w2),
        "fbias": f32(bias).reshape(1, D_MODEL),
        "ident_in": np.eye(128, dtype=np.float32).astype(ml_dtypes.bfloat16),
    }
    in_maps = []
    for c in range(8):
        b, q = c // 4, c % 4
        t0 = q * T_OWN
        mask = np.zeros((1, max(T_PRE, 1)), dtype=np.float32)
        mask[0, :min(t0, T_PRE)] = -1.0
        prev8 = x[b, t0 - 8:t0] if t0 >= 8 else np.zeros((8, D), dtype=np.float32)
        xo = np.ascontiguousarray(np.concatenate([prev8, x[b, t0:t0 + T_OWN]], axis=0))
        m = dict(base)
        m["xpre"] = np.ascontiguousarray(x[b, :T_PRE]) if T_PRE > 0 else \
            np.zeros((1, D), dtype=np.float32)
        m["xown"] = xo
        m["maskpre"] = mask
        in_maps.append(m)
    return in_maps


def kernel(L0_attn_x, L0_attn_norm_w, L0_attn_W_in, L0_attn_conv_w, L0_attn_conv_b,
           L0_attn_W_x, L0_attn_w_dt, L0_attn_b_dt, L0_attn_A_log, L0_attn_D_param,
           L0_attn_W_out, L0_ffn_router_w, L0_ffn_w1, L0_ffn_v1, L0_ffn_w2,
           L0_ffn_bias):
    cfg = default_cfg()
    x = np.asarray(L0_attn_x, dtype=np.float32)
    B, T, D = x.shape
    nc = build_program(cfg)
    in_maps = stage_inputs(cfg, x, L0_attn_norm_w, L0_attn_W_in, L0_attn_conv_w,
                           L0_attn_conv_b, L0_attn_W_x, L0_attn_w_dt, L0_attn_b_dt,
                           L0_attn_A_log, L0_attn_D_param, L0_attn_W_out,
                           L0_ffn_router_w, L0_ffn_w1, L0_ffn_v1, L0_ffn_w2,
                           L0_ffn_bias)
    res = run_bass_kernel_spmd(nc, in_maps, core_ids=list(range(8)))
    T_OWN = cfg["T_OWN"]
    out = np.empty((B, T, D), dtype=np.float32)
    for c in range(8):
        b, q = c // 4, c % 4
        out[b, q * T_OWN:(q + 1) * T_OWN] = res.results[c]["out"]
    return out


# version 06
# speedup vs baseline: 1.1503x; 1.0403x over previous
import numpy as np
import ml_dtypes
from contextlib import ExitStack
import concourse.bass as bass
import concourse.mybir as mybir
from concourse.bass_utils import run_bass_kernel_spmd

F32 = mybir.dt.float32
BF16 = mybir.dt.bfloat16
AF = mybir.ActivationFunctionType
ALU = mybir.AluOpType
AX = mybir.AxisListType

D_MODEL = 1024
D_INNER = 2048
D_FF = 2816
NK = D_MODEL // 128
NJ = D_INNER // 128
NF = D_FF // 128
NE = 8
TS = 512
FFW = TS + 3
LN_EPS = 1e-5
RMS_EPS = 1e-6


def default_cfg():
    return dict(T_PRE=12288, T_OWN=4096)


class Emitter:
    ENGINES = ("t", "v", "s", "g", "sy")

    def __init__(self, nc):
        self.nc = nc
        self.q = {e: [] for e in self.ENGINES}
        self.sems = {}
        self.cnt = {}

    def sem(self, name):
        if name not in self.sems:
            self.sems[name] = self.nc.alloc_semaphore(name=name)
            self.cnt[name] = 0
        return self.sems[name]

    def op(self, qname, fn, inc=None):
        if inc is None:
            self.q[qname].append(fn)
            return None
        sem_name, by = inc
        h = self.sem(sem_name)
        self.cnt[sem_name] += by
        after = self.cnt[sem_name]

        def wrapped(e, fn=fn, h=h, by=by):
            fn(e).then_inc(h, by)
        self.q[qname].append(wrapped)
        return after

    def wait(self, qname, sem_name, val):
        if val is None or val <= 0:
            return
        h = self.sem(sem_name)
        self.q[qname].append(lambda e, h=h, v=val: e.wait_ge(h, v))

    def dma(self, qname, out, in_, sem_name):
        self.wait(qname, sem_name, self.cnt[sem_name])
        return self.op(qname, lambda e, o=out, i=in_: e.dma_start(out=o, in_=i),
                       inc=(sem_name, 16))

    def vstep(self, fn):
        n = self.op("v", fn, inc=("vS", 1))
        self.wait("v", "vS", n)
        return n

    def sstep(self, fn):
        n = self.op("s", fn, inc=("sS", 1))
        self.wait("s", "sS", n)
        return n

    def replay(self, engine, qname):
        for fn in self.q[qname]:
            fn(engine)


def build_program(cfg):
    T_PRE = cfg["T_PRE"]
    T_OWN = cfg["T_OWN"]
    NSUB_PRE = T_PRE // TS
    NSUB_OWN = T_OWN // TS
    NT_OWN = T_OWN // 128

    nc = bass.Bass("TRN2", target_bir_lowering=False, dynamic_dma_scratch_size=16384,
                   detect_race_conditions=False)

    xpre = nc.dram_tensor("xpre", [max(T_PRE, 1), D_MODEL], F32, kind="ExternalInput")
    xown = nc.dram_tensor("xown", [8 + T_OWN, D_MODEL], F32, kind="ExternalInput")
    maskpre = nc.dram_tensor("maskpre", [1, max(T_PRE, 1)], F32, kind="ExternalInput")
    win = nc.dram_tensor("win", [D_MODEL, 2 * D_INNER], F32, kind="ExternalInput")
    wout = nc.dram_tensor("wout", [D_INNER, D_MODEL], F32, kind="ExternalInput")
    wx_r = nc.dram_tensor("wx_r", [128, NJ * 33], F32, kind="ExternalInput")
    alog_r = nc.dram_tensor("alog_r", [128, NJ * 16], F32, kind="ExternalInput")
    wdt_r = nc.dram_tensor("wdt_r", [128, NJ], F32, kind="ExternalInput")
    bdt_r = nc.dram_tensor("bdt_r", [128, NJ], F32, kind="ExternalInput")
    cw_r = nc.dram_tensor("cw_r", [128, NJ * 4], F32, kind="ExternalInput")
    cb_r = nc.dram_tensor("cb_r", [128, NJ], F32, kind="ExternalInput")
    dD_r = nc.dram_tensor("dD_r", [128, NJ], F32, kind="ExternalInput")
    normw_r = nc.dram_tensor("normw_r", [128, NK], F32, kind="ExternalInput")
    rw_r = nc.dram_tensor("rw_r", [128, NK * NE], F32, kind="ExternalInput")
    w1 = nc.dram_tensor("w1", [NE, D_MODEL, D_FF], F32, kind="ExternalInput")
    v1 = nc.dram_tensor("v1", [NE, D_MODEL, D_FF], F32, kind="ExternalInput")
    w2 = nc.dram_tensor("w2", [NE, D_FF, D_MODEL], F32, kind="ExternalInput")
    fbias = nc.dram_tensor("fbias", [1, D_MODEL], F32, kind="ExternalInput")
    ident_in = nc.dram_tensor("ident_in", [128, 128], BF16, kind="ExternalInput")

    out_t = nc.dram_tensor("out", [T_OWN, D_MODEL], F32, kind="ExternalOutput")

    dbg = cfg.get("dbg", False)
    if dbg:
        dbg_t = {
            "dbg_xp": nc.dram_tensor("dbg_xp", [128, NJ * FFW], F32,
                                     kind="ExternalOutput"),
            "dbg_xc": nc.dram_tensor("dbg_xc", [128, NJ * TS], F32,
                                     kind="ExternalOutput"),
            "dbg_dt": nc.dram_tensor("dbg_dt", [128, NJ * TS], F32,
                                     kind="ExternalOutput"),
            "dbg_carry": nc.dram_tensor("dbg_carry", [128, NJ * 16], F32,
                                        kind="ExternalOutput"),
            "dbg_ssm": nc.dram_tensor("dbg_ssm", [33, TS], F32,
                                      kind="ExternalOutput"),
            "dbg_a2t": nc.dram_tensor("dbg_a2t", [128, NK * TS], BF16,
                                      kind="ExternalOutput"),
            "dbg_y": nc.dram_tensor("dbg_y", [128, NJ * TS], F32,
                                    kind="ExternalOutput"),
            "dbg_m": nc.dram_tensor("dbg_m", [128, 64 * NE], F32,
                                    kind="ExternalOutput"),
            "dbg_sc": nc.dram_tensor("dbg_sc", [128, 64 * NE], F32,
                                     kind="ExternalOutput"),
            "dbg_s1": nc.dram_tensor("dbg_s1", [128, 64], F32,
                                     kind="ExternalOutput"),
            "dbg_b1": nc.dram_tensor("dbg_b1", [128, 64], F32,
                                     kind="ExternalOutput"),
            "dbg_a1t": nc.dram_tensor("dbg_a1t", [128, NK * TS], BF16,
                                      kind="ExternalOutput"),
            "dbg_oi": nc.dram_tensor("dbg_oi", [128, 4 * D_MODEL], F32,
                                     kind="ExternalOutput"),
            "dbg_asb": nc.dram_tensor("dbg_asb", [128, NJ * 16], F32,
                                      kind="ExternalOutput"),
            "dbg_abxh": nc.dram_tensor("dbg_abxh", [128, 3 * TS], F32,
                                       kind="ExternalOutput"),
        }
    ygb_scr = nc.dram_tensor("ygb_scr", [NJ, 128, T_OWN], BF16, kind="Internal")
    ssm_dram = nc.dram_tensor("ssm_dram", [33, TS], F32, kind="Internal")
    hb_scr = nc.dram_tensor("hb_scr", [NE, NF, 128, T_OWN], BF16, kind="Internal")

    E = Emitter(nc)
    for s in ("dS", "dA", "dG", "dB0", "dB1", "dV", "dH0", "dH1",
              "tS", "vS", "sS"):
        E.sem(s)

    es0 = ExitStack()
    sb0 = lambda n, sh, dt=F32: es0.enter_context(nc.sbuf_tensor(n, sh, dt))

    ident = sb0("ident", [128, 128], BF16)
    a_sb = sb0("a_sb", [128, NJ * 16], F32)
    wdt_sb = sb0("wdt_sb", [128, NJ], F32)
    bdt_sb = sb0("bdt_sb", [128, NJ], F32)
    cw_sb = sb0("cw_sb", [128, NJ * 4], F32)
    cb_sb = sb0("cb_sb", [128, NJ], F32)
    dD_sb = sb0("dD_sb", [128, NJ], F32)
    normw_sb = sb0("normw_sb", [128, NK], F32)
    rw_sb = sb0("rw_sb", [128, NK * NE], BF16)
    carry = sb0("carry", [128, NJ * 16], F32)
    s1_all = sb0("s1_all", [128, NT_OWN], F32)
    b1_all = sb0("b1_all", [128, NT_OWN], F32)
    m_sb = sb0("m_sb", [128, NT_OWN * NE], F32)
    sc_sb = sb0("sc_sb", [128, NT_OWN * NE], F32)
    st = sb0("st", [128, 64], F32)
    eps_sb = sb0("eps_sb", [128, 3], F32)
    tmp_buf = sb0("tmp_buf", [128, TS], F32)

    E.dma("sy", ident[:, :], ident_in.ap(), "dS")
    E.dma("sy", a_sb[:, :], alog_r.ap(), "dS")
    E.dma("sy", wdt_sb[:, :], wdt_r.ap(), "dS")
    E.dma("sy", bdt_sb[:, :], bdt_r.ap(), "dS")
    E.dma("sy", cw_sb[:, :], cw_r.ap(), "dS")
    E.dma("sy", cb_sb[:, :], cb_r.ap(), "dS")
    E.dma("sy", dD_sb[:, :], dD_r.ap(), "dS")
    d_setup = E.dma("sy", normw_sb[:, :], normw_r.ap(), "dS")
    E.dma("g", rw_sb[:, :], rw_r.ap(), "dG")
    d_gsetup0 = E.cnt["dG"]

    E.op("v", lambda e: e.memset(eps_sb[:, 0:1], LN_EPS))
    E.op("v", lambda e: e.memset(eps_sb[:, 1:2], RMS_EPS))
    v_eps = E.op("v", lambda e: e.memset(eps_sb[:, 2:3], -0.5), inc=("vS", 1))
    E.wait("s", "dS", d_setup)
    E.wait("s", "vS", v_eps)
    s_a1 = E.op("s", lambda e: e.activation(a_sb[:, :], a_sb[:, :], AF.Exp),
                inc=("sS", 1))
    E.wait("v", "sS", s_a1)
    E.vstep(lambda e: e.memset(carry[:, :], 0.0))

    es1 = ExitStack()
    sb1 = lambda n, sh, dt=F32: es1.enter_context(nc.sbuf_tensor(n, sh, dt))
    ps1 = lambda n, sh, dt=F32: es1.enter_context(nc.psum_tensor(n, sh, dt))

    win_x = sb1("win_x", [128, NK * D_INNER], BF16)
    wx_sb = sb1("wx_sb", [128, NJ * 33], BF16)
    x_sb = sb1("x_sb", [128, D_MODEL], F32)
    a2_sb = sb1("a2_sb", [128, D_MODEL], BF16)
    a2t = sb1("a2t", [128, NK * TS], BF16)
    xp_sb = sb1("xp_sb", [128, NJ * FFW], F32)
    xc_sb = sb1("xc_sb", [128, NJ * TS], F32)
    dt_sb = sb1("dt_sb", [128, NJ * TS], F32)
    y_sb = sb1("y_sb", [128, NJ * TS], F32)
    tail_sb = sb1("tail_sb", [128, NJ * 3], F32)
    ssm_sb = sb1("ssm_sb", [33, TS], F32)
    r_bc = sb1("r_bc", [128, TS], F32)
    b_bc = sb1("b_bc", [128, 2 * TS], F32)
    c_bc = sb1("c_bc", [128, 2 * TS], F32)
    mask_bc = c_bc
    a_buf = sb1("a_buf", [128, 2 * TS], F32)
    bx_buf = sb1("bx_buf", [128, TS], F32)
    h_buf = sb1("h_buf", [128, TS], F32)
    xcb_buf = sb1("xcb_buf", [128, 2 * TS], BF16)
    zpan = sb1("zpan", [128, NK * 128], BF16)
    ygst = sb1("ygst", [128, TS], BF16)

    tp_ps = [ps1(f"tp_ps{i}", [128, 1024], BF16) for i in range(2)]
    mm_ps = [ps1(f"mm_ps{i}", [128, 512], F32) for i in range(2)]
    z_ps = [ps1(f"z_ps{i}", [128, 512], F32) for i in range(2)]
    ssm_ps = ps1("ssm_ps", [33, TS], F32)
    pad_ps = ps1("pad_ps", [128, 512], F32)

    E.dma("g", wx_sb[:, :], wx_r.ap(), "dG")
    d_gsetup = E.dma("g", win_x[:, :].rearrange("p (k c) -> p k c", k=NK),
                     win.ap()[:, 0:D_INNER].rearrange("(k p) c -> p k c", p=128),
                     "dG")
    vv = E.op("v", lambda e: e.memset(xp_sb[:, :], 0.0), inc=("vS", 1))
    E.wait("t", "dG", d_gsetup)
    E.wait("t", "vS", vv)

    state = dict(mm_war=[None, None], z_war=[None, None], tp_war=[None, None],
                 abuf_war=[None, None], a2t_use_t=0, a2sb_use_t=0,
                 ygst_war=0, xsb_use=("vS", 0), ssm_bc0=0, ssm_bc1=0, ssm_sd=0,
                 rbc_use=0, maskbc_use=0)

    def c_(i, k):
        return st[:, 14 * i + k:14 * i + k + 1]

    def load_stats_a2_transpose(own, k, xsrc, xoff):
        sev = None
        for i in range(4):
            E.wait("sy", *state["xsb_use"])
            d_x = E.dma("sy", x_sb[:, :],
                        xsrc.ap()[xoff + i * 128:xoff + (i + 1) * 128, :], "dS")
            E.wait("v", "dS", d_x)
            xt = x_sb[:, :]
            E.op("v", lambda e, xt=xt, i=i: e.reduce_sum(c_(i, 0), xt, axis=AX.X))
            E.op("v", lambda e, xt=xt: e.tensor_tensor(
                tmp_buf[:, :], xt[:, 0:512], xt[:, 0:512], op=ALU.mult))
            E.vstep(lambda e, i=i: e.reduce_sum(c_(i, 1), tmp_buf[:, :], axis=AX.X))
            E.op("v", lambda e, xt=xt: e.tensor_tensor(
                tmp_buf[:, :], xt[:, 512:1024], xt[:, 512:1024], op=ALU.mult))
            E.vstep(lambda e, i=i: e.reduce_sum(c_(i, 2), tmp_buf[:, :], axis=AX.X))
            E.vstep(lambda e, i=i: e.tensor_scalar(
                out=c_(i, 3), in0=c_(i, 0), scalar1=1.0 / D_MODEL, scalar2=None,
                op0=ALU.mult))
            E.vstep(lambda e, i=i: e.tensor_tensor(c_(i, 4), c_(i, 1), c_(i, 2),
                                                   op=ALU.add))
            E.vstep(lambda e, i=i: e.tensor_scalar(
                out=c_(i, 4), in0=c_(i, 4), scalar1=1.0 / D_MODEL, scalar2=None,
                op0=ALU.mult))
            E.vstep(lambda e, i=i: e.tensor_tensor(c_(i, 5), c_(i, 3), c_(i, 3),
                                                   op=ALU.mult))
            lv = E.vstep(lambda e, i=i: e.tensor_tensor(c_(i, 6), c_(i, 4), c_(i, 5),
                                                        op=ALU.subtract))
            E.wait("s", "vS", lv)
            E.sstep(lambda e, i=i: e.activation(c_(i, 7), c_(i, 6), AF.Ln,
                                                bias=eps_sb[:, 0:1]))
            sl = E.sstep(lambda e, i=i: e.activation(c_(i, 8), c_(i, 7), AF.Exp,
                                                     scale=eps_sb[:, 2:3]))
            E.wait("v", "sS", sl)
            E.vstep(lambda e, i=i: e.tensor_tensor(c_(i, 9), c_(i, 8), c_(i, 8),
                                                   op=ALU.mult))
            lv = E.vstep(lambda e, i=i: e.tensor_tensor(c_(i, 9), c_(i, 6), c_(i, 9),
                                                        op=ALU.mult))
            E.wait("s", "vS", lv)
            E.sstep(lambda e, i=i: e.activation(c_(i, 10), c_(i, 9), AF.Ln,
                                                bias=eps_sb[:, 1:2]))
            sl = E.sstep(lambda e, i=i: e.activation(c_(i, 11), c_(i, 10), AF.Exp,
                                                     scale=eps_sb[:, 2:3]))
            E.wait("v", "sS", sl)
            E.vstep(lambda e, i=i: e.tensor_tensor(c_(i, 12), c_(i, 8), c_(i, 11),
                                                   op=ALU.mult))
            lv = E.vstep(lambda e, i=i: e.tensor_tensor(c_(i, 13), c_(i, 3), c_(i, 12),
                                                        op=ALU.mult))
            lv = E.vstep(lambda e, i=i: e.tensor_scalar(
                out=c_(i, 13), in0=c_(i, 13), scalar1=-1.0, scalar2=None,
                op0=ALU.mult))
            if own:
                bt = k * 4 + i
                E.vstep(lambda e, i=i, bt=bt: e.tensor_copy(
                    s1_all[:, bt:bt + 1], c_(i, 8)))
                E.vstep(lambda e, i=i: e.tensor_tensor(
                    st[:, 60:61], c_(i, 3), c_(i, 8), op=ALU.mult))
                lv = E.vstep(lambda e, i=i, bt=bt: e.tensor_scalar(
                    out=b1_all[:, bt:bt + 1], in0=st[:, 60:61], scalar1=-1.0,
                    scalar2=None, op0=ALU.mult))
            E.wait("v", "tS", state["a2sb_use_t"])
            sl = E.op("v", lambda e, i=i: e.tensor_scalar(
                out=a2_sb[:, :], in0=x_sb[:, :], scalar1=c_(i, 12),
                scalar2=c_(i, 13), op0=ALU.mult, op1=ALU.add), inc=("vS", 1))
            state["xsb_use"] = ("vS", sl)
            E.wait("t", "vS", sl)
            for kk in range(NK):
                bank = kk % 2
                if state["tp_war"][bank] is not None:
                    E.wait("t", "sS", state["tp_war"][bank])
                tl = E.op("t", lambda e, kk=kk, bank=bank: e.transpose(
                    tp_ps[bank][:, 0:128], a2_sb[:, kk * 128:(kk + 1) * 128],
                    ident[:, :]), inc=("tS", 1))
                state["a2sb_use_t"] = tl
                E.wait("s", "tS", tl)
                E.wait("s", "tS", state["a2t_use_t"])
                sev = E.op("s", lambda e, kk=kk, i=i, bank=bank: e.activation(
                    a2t[:, kk * TS + i * 128:kk * TS + (i + 1) * 128],
                    tp_ps[bank][:, 0:128], AF.Copy,
                    scale=normw_sb[:, kk:kk + 1]), inc=("sS", 1))
                state["tp_war"][bank] = sev
        return sev

    def mamba_subchunk(k, own, xsrc, xoff, mask_off, skip_tail):
        if mask_off is not None:
            E.wait("sy", "vS", state["maskbc_use"])
            d_mask = E.dma("sy", mask_bc[:, 0:TS],
                           maskpre.ap()[0:1, mask_off:mask_off + TS]
                           .broadcast_to([128, TS]), "dS")
        a2t_ready = load_stats_a2_transpose(own, k, xsrc, xoff)

        if not skip_tail:
            for j in range(NJ):
                E.op("v", lambda e, j=j: e.tensor_copy(
                    tail_sb[:, j * 3:(j + 1) * 3],
                    xp_sb[:, j * FFW + TS:j * FFW + TS + 3]))
            E.vstep(lambda e: e.engine_nop())
            for j in range(NJ):
                E.op("v", lambda e, j=j: e.tensor_copy(
                    xp_sb[:, j * FFW:j * FFW + 3], tail_sb[:, j * 3:(j + 1) * 3]))
        v_tail = E.op("v", lambda e: e.engine_nop(), inc=("vS", 1))

        E.wait("t", "sS", a2t_ready)
        for j in range(NJ):
            war = state["mm_war"][j % 2]
            if war is not None:
                E.wait("t", "vS", war)
            mml = None
            for kk in range(NK):
                mml = E.op("t", lambda e, j=j, kk=kk: e.matmul(
                    mm_ps[j % 2][:, :],
                    win_x[:, kk * D_INNER + j * 128:kk * D_INNER + (j + 1) * 128],
                    a2t[:, kk * TS:(kk + 1) * TS],
                    start=(kk == 0), stop=(kk == NK - 1)),
                    inc=("tS", 1) if kk == NK - 1 else None)
            state["a2t_use_t"] = mml
            E.wait("v", "tS", mml)
            if j == 0:
                E.wait("v", "vS", v_tail)
            ev = E.op("v", lambda e, j=j: e.tensor_copy(
                xp_sb[:, j * FFW + 3:j * FFW + 3 + TS], mm_ps[j % 2][:, :]),
                inc=("vS", 1))
            state["mm_war"][j % 2] = ev

        s_silu = None
        for j in range(NJ):
            xcj = xc_sb[:, j * TS:(j + 1) * TS]
            E.op("v", lambda e, j=j, xcj=xcj: e.tensor_scalar(
                out=xcj, in0=xp_sb[:, j * FFW:j * FFW + TS],
                scalar1=cw_sb[:, j * 4:j * 4 + 1], scalar2=None, op0=ALU.mult))
            for kk in range(1, 4):
                E.op("v", lambda e, j=j, kk=kk, xcj=xcj: e.scalar_tensor_tensor(
                    xcj, xp_sb[:, j * FFW + kk:j * FFW + kk + TS],
                    cw_sb[:, j * 4 + kk:j * 4 + kk + 1], xcj,
                    op0=ALU.mult, op1=ALU.add))
            cv = E.op("v", lambda e: e.engine_nop(), inc=("vS", 1))
            E.wait("s", "vS", cv)
            s_silu = E.op("s", lambda e, xcj=xcj, j=j: e.activation(
                xcj, xcj, AF.Silu, bias=cb_sb[:, j:j + 1]), inc=("sS", 1))
            if own:
                E.wait("v", "sS", s_silu)
                E.op("v", lambda e, j=j, xcj=xcj: e.tensor_scalar(
                    out=y_sb[:, j * TS:(j + 1) * TS], in0=xcj,
                    scalar1=dD_sb[:, j:j + 1], scalar2=None, op0=ALU.mult))

        E.wait("v", "sS", s_silu)
        E.wait("v", "dG", state["ssm_sd"])
        mml = None
        for j in range(NJ):
            if j >= 2:
                E.wait("v", "tS", state[f"wx_mm{j % 2}"])
            cb = E.op("v", lambda e, j=j: e.tensor_copy(
                xcb_buf[:, (j % 2) * TS:(j % 2 + 1) * TS],
                xc_sb[:, j * TS:(j + 1) * TS]), inc=("vS", 1))
            E.wait("t", "vS", cb)
            mml = E.op("t", lambda e, j=j: e.matmul(
                ssm_ps[:, :], wx_sb[:, j * 33:(j + 1) * 33],
                xcb_buf[:, (j % 2) * TS:(j % 2 + 1) * TS],
                start=(j == 0), stop=(j == NJ - 1)), inc=("tS", 1))
            state[f"wx_mm{j % 2}"] = mml
        E.wait("v", "tS", mml)
        v_ssm = E.op("v", lambda e: e.tensor_copy(ssm_sb[:, :], ssm_ps[:, :]),
                     inc=("vS", 1))
        E.wait("t", "vS", v_ssm)
        E.wait("g", "vS", v_ssm)
        E.wait("g", "dB0", state["ssm_bc0"])
        E.wait("g", "dB1", state["ssm_bc1"])
        d_sd = E.dma("g", ssm_dram.ap(), ssm_sb[:, :], "dG")
        state["ssm_sd"] = d_sd
        E.wait("g", "sS", state["rbc_use"])
        d_r = E.dma("g", r_bc[:, :],
                    ssm_dram.ap()[0:1, :].broadcast_to([128, TS]), "dG")

        E.wait("s", "dG", d_r)
        v_dt = None
        for j in range(NJ):
            dtj = dt_sb[:, j * TS:(j + 1) * TS]
            E.op("s", lambda e, j=j, dtj=dtj: e.activation(
                dtj, r_bc[:, :], AF.Relu, scale=wdt_sb[:, j:j + 1],
                bias=bdt_sb[:, j:j + 1]))
            E.op("s", lambda e, j=j: e.activation(
                tmp_buf[:, :], r_bc[:, :], AF.Abs, scale=wdt_sb[:, j:j + 1],
                bias=bdt_sb[:, j:j + 1]))
            E.op("s", lambda e: e.activation(tmp_buf[:, :], tmp_buf[:, :], AF.Exp,
                                             scale=-1.0))
            sl = E.op("s", lambda e: e.activation(tmp_buf[:, :], tmp_buf[:, :], AF.Ln,
                                                  bias=1.0), inc=("sS", 1))
            E.wait("v", "sS", sl)
            if mask_off is not None and j == 0:
                E.wait("v", "dS", d_mask)
            v_dt = E.op("v", lambda e, dtj=dtj: e.tensor_tensor(
                dtj, dtj, tmp_buf[:, :], op=ALU.add), inc=("vS", 1))
            if mask_off is not None:
                v_dt = E.op("v", lambda e, dtj=dtj: e.tensor_tensor(
                    dtj, dtj, mask_bc[:, 0:TS], op=ALU.mult), inc=("vS", 1))
            else:
                v_dt = E.op("v", lambda e, dtj=dtj: e.tensor_scalar(
                    out=dtj, in0=dtj, scalar1=-1.0, scalar2=None, op0=ALU.mult),
                    inc=("vS", 1))
            E.wait("s", "vS", v_dt)
        state["rbc_use"] = E.cnt["sS"]
        state["maskbc_use"] = v_dt if mask_off is not None else state["maskbc_use"]

        v_u = None
        for j in range(NJ):
            v_u = E.op("v", lambda e, j=j: e.tensor_tensor(
                xc_sb[:, j * TS:(j + 1) * TS], xc_sb[:, j * TS:(j + 1) * TS],
                dt_sb[:, j * TS:(j + 1) * TS], op=ALU.mult), inc=("vS", 1))

        s_gz = {}
        if own:
            for j in range(NJ):
                E.op("v", lambda e, j=j: e.tensor_copy(
                    tail_sb[:, j * 3:(j + 1) * 3],
                    xp_sb[:, j * FFW + TS:j * FFW + TS + 3]))
            v_tail2 = E.op("v", lambda e: e.engine_nop(), inc=("vS", 1))
            zp_use = 0
            for m in range(NJ):
                E.wait("g", "tS", zp_use)
                d_zp = E.dma(
                    "g", zpan[:, :].rearrange("p (k c) -> p k c", k=NK),
                    win.ap()[:, D_INNER + m * 128:D_INNER + (m + 1) * 128]
                    .rearrange("(k p) c -> p k c", p=128), "dG")
                war = state["z_war"][m % 2]
                if war is not None:
                    E.wait("t", "sS", war)
                E.wait("t", "dG", d_zp)
                mml = None
                for kk in range(NK):
                    mml = E.op("t", lambda e, m=m, kk=kk: e.matmul(
                        z_ps[m % 2][:, :],
                        zpan[:, kk * 128:(kk + 1) * 128],
                        a2t[:, kk * TS:(kk + 1) * TS],
                        start=(kk == 0), stop=(kk == NK - 1)), inc=("tS", 1))
                zp_use = mml
                state["a2t_use_t"] = mml
                E.wait("s", "tS", mml)
                if m == 0:
                    E.wait("s", "vS", v_tail2)
                sg = E.op("s", lambda e, m=m: e.activation(
                    xp_sb[:, m * FFW:m * FFW + TS], z_ps[m % 2][:, :], AF.Silu),
                    inc=("sS", 1))
                state["z_war"][m % 2] = sg
                s_gz[m] = sg

        E.wait("s", "vS", v_u)
        for sig in range(16):
            par = sig % 2
            E.wait("g", "vS", state.get(f"bbc_use{par}", 0))
            d_b = E.dma("g", b_bc[:, par * TS:(par + 1) * TS],
                        ssm_dram.ap()[1 + sig:2 + sig, :].broadcast_to([128, TS]),
                        f"dB{par}")
            if own:
                d_c = E.dma("g", c_bc[:, par * TS:(par + 1) * TS],
                            ssm_dram.ap()[17 + sig:18 + sig, :]
                            .broadcast_to([128, TS]), f"dB{par}")
            for j in range(NJ):
                col = j * 16 + sig
                pb = (sig * NJ + j) % 2
                war = state["abuf_war"][pb]
                if war is not None:
                    E.wait("s", "vS", war)
                sa = E.op("s", lambda e, j=j, col=col, pb=pb: e.activation(
                    a_buf[:, pb * TS:(pb + 1) * TS], dt_sb[:, j * TS:(j + 1) * TS],
                    AF.Exp, scale=a_sb[:, col:col + 1]), inc=("sS", 1))
                E.wait("v", "sS", sa)
                if j == 0:
                    E.wait("v", f"dB{par}", d_c if own else d_b)
                E.op("v", lambda e, j=j, par=par: e.tensor_tensor(
                    bx_buf[:, :], xc_sb[:, j * TS:(j + 1) * TS],
                    b_bc[:, par * TS:(par + 1) * TS], op=ALU.mult))
                sc = E.op("v", lambda e, col=col, pb=pb: e.tensor_tensor_scan(
                    h_buf[:, :], a_buf[:, pb * TS:(pb + 1) * TS], bx_buf[:, :],
                    carry[:, col:col + 1], op0=ALU.mult, op1=ALU.add),
                    inc=("vS", 1))
                state["abuf_war"][pb] = sc
                if not own:
                    E.wait("v", "vS", sc)
                if dbg and own and sig == 0 and j == 0 and (xoff - 8) // TS == 0:
                    E.wait("s", "vS", sc)
                    E.dma("s", dbg_t["dbg_asb"].ap(), a_sb[:, :], "dA")
                    E.dma("s", dbg_t["dbg_abxh"].ap()[:, 0:TS],
                          a_buf[:, pb * TS:(pb + 1) * TS], "dA")
                    E.dma("s", dbg_t["dbg_abxh"].ap()[:, TS:2 * TS], bx_buf[:, :],
                          "dA")
                    E.dma("s", dbg_t["dbg_abxh"].ap()[:, 2 * TS:3 * TS], h_buf[:, :],
                          "dA")
                    dbgd = E.cnt["dA"]
                    E.wait("v", "dA", dbgd)
                if own:
                    E.op("v", lambda e, par=par: e.tensor_tensor(
                        tmp_buf[:, :], h_buf[:, :],
                        c_bc[:, par * TS:(par + 1) * TS], op=ALU.mult))
                    E.op("v", lambda e, j=j: e.tensor_tensor(
                        y_sb[:, j * TS:(j + 1) * TS], y_sb[:, j * TS:(j + 1) * TS],
                        tmp_buf[:, :], op=ALU.subtract))
                E.op("v", lambda e, col=col: e.tensor_copy(
                    carry[:, col:col + 1], h_buf[:, TS - 1:TS]))
            lastv = E.op("v", lambda e: e.engine_nop(), inc=("vS", 1))
            state[f"bbc_use{par}"] = lastv
        state["ssm_bc0"] = E.cnt["dB0"]
        state["ssm_bc1"] = E.cnt["dB1"]

        if own:
            ksub = (xoff - 8) // TS
            if dbg and ksub == 0:
                E.wait("s", "vS", E.cnt["vS"])
                E.wait("s", "sS", E.cnt["sS"])
                for nm, sbuf in (("dbg_xp", xp_sb), ("dbg_xc", xc_sb),
                                 ("dbg_dt", dt_sb), ("dbg_carry", carry),
                                 ("dbg_ssm", ssm_sb), ("dbg_a2t", a2t),
                                 ("dbg_y", y_sb)):
                    E.dma("s", dbg_t[nm].ap(), sbuf[:, :], "dA")
                dbg_done = E.cnt["dA"]
                for qn in ("v", "g", "sy", "t"):
                    E.wait(qn, "dA", dbg_done)
                E.wait("s", "dA", dbg_done)
            for j in range(NJ):
                E.wait("v", "sS", s_gz[j])
                E.op("v", lambda e, j=j: e.tensor_tensor(
                    y_sb[:, j * TS:(j + 1) * TS], y_sb[:, j * TS:(j + 1) * TS],
                    xp_sb[:, j * FFW:j * FFW + TS], op=ALU.mult))
                E.wait("v", "dA", state["ygst_war"])
                gv = E.op("v", lambda e, j=j: e.tensor_copy(
                    ygst[:, :], y_sb[:, j * TS:(j + 1) * TS]), inc=("vS", 1))
                E.wait("s", "vS", gv)
                dd = E.dma("s", ygb_scr.ap()[j, :, ksub * TS:(ksub + 1) * TS],
                           ygst[:, :], "dA")
                state["ygst_war"] = dd

    for k in range(NSUB_PRE):
        mamba_subchunk(k, own=False, xsrc=xpre, xoff=k * TS, mask_off=k * TS,
                       skip_tail=(k == 0))


    E.wait("sy", *state["xsb_use"])
    d_pad = E.dma("sy", x_sb[0:8, 0:D_MODEL], xown.ap()[0:8, :], "dS")
    E.wait("v", "dS", d_pad)
    pc = lambda k: st[0:8, k:k + 1]
    E.op("v", lambda e: e.reduce_sum(pc(0), x_sb[0:8, 0:D_MODEL], axis=AX.X))
    E.op("v", lambda e: e.tensor_tensor(tmp_buf[0:8, :], x_sb[0:8, 0:512],
                                        x_sb[0:8, 0:512], op=ALU.mult))
    E.vstep(lambda e: e.reduce_sum(pc(1), tmp_buf[0:8, :], axis=AX.X))
    E.op("v", lambda e: e.tensor_tensor(tmp_buf[0:8, :], x_sb[0:8, 512:1024],
                                        x_sb[0:8, 512:1024], op=ALU.mult))
    E.vstep(lambda e: e.reduce_sum(pc(2), tmp_buf[0:8, :], axis=AX.X))
    E.vstep(lambda e: e.tensor_scalar(out=pc(3), in0=pc(0), scalar1=1.0 / D_MODEL,
                                      scalar2=None, op0=ALU.mult))
    E.vstep(lambda e: e.tensor_tensor(pc(4), pc(1), pc(2), op=ALU.add))
    E.vstep(lambda e: e.tensor_scalar(out=pc(4), in0=pc(4), scalar1=1.0 / D_MODEL,
                                      scalar2=None, op0=ALU.mult))
    E.vstep(lambda e: e.tensor_tensor(pc(5), pc(3), pc(3), op=ALU.mult))
    pv = E.vstep(lambda e: e.tensor_tensor(pc(6), pc(4), pc(5), op=ALU.subtract))
    E.wait("s", "vS", pv)
    E.sstep(lambda e: e.activation(pc(7), pc(6), AF.Ln, bias=eps_sb[0:8, 0:1]))
    sx = E.sstep(lambda e: e.activation(pc(8), pc(7), AF.Exp,
                                        scale=eps_sb[0:8, 2:3]))
    E.wait("v", "sS", sx)
    E.vstep(lambda e: e.tensor_tensor(pc(9), pc(8), pc(8), op=ALU.mult))
    pv = E.vstep(lambda e: e.tensor_tensor(pc(9), pc(6), pc(9), op=ALU.mult))
    E.wait("s", "vS", pv)
    E.sstep(lambda e: e.activation(pc(10), pc(9), AF.Ln, bias=eps_sb[0:8, 1:2]))
    sx = E.sstep(lambda e: e.activation(pc(11), pc(10), AF.Exp,
                                        scale=eps_sb[0:8, 2:3]))
    E.wait("v", "sS", sx)
    E.vstep(lambda e: e.tensor_tensor(pc(12), pc(8), pc(11), op=ALU.mult))
    E.vstep(lambda e: e.tensor_tensor(pc(13), pc(3), pc(12), op=ALU.mult))
    pv = E.vstep(lambda e: e.tensor_scalar(out=pc(13), in0=pc(13), scalar1=-1.0,
                                           scalar2=None, op0=ALU.mult))
    E.wait("v", "tS", state["a2sb_use_t"])
    sx = E.op("v", lambda e: e.tensor_scalar(
        out=a2_sb[0:8, 0:D_MODEL], in0=x_sb[0:8, 0:D_MODEL], scalar1=pc(12),
        scalar2=pc(13), op0=ALU.mult, op1=ALU.add), inc=("vS", 1))
    state["xsb_use"] = ("vS", sx)
    E.wait("t", "vS", sx)
    tl = None
    for kk in range(NK):
        war = state["tp_war"][0]
        if war is not None and kk == 0:
            E.wait("t", "sS", war)
        tl = E.op("t", lambda e, kk=kk: e.transpose(
            tp_ps[0][:, kk * 8:(kk + 1) * 8], a2_sb[0:8, kk * 128:(kk + 1) * 128],
            ident[0:8, 0:8]), inc=("tS", 1))
    state["a2sb_use_t"] = tl
    E.wait("s", "tS", tl)
    E.wait("s", "tS", state["a2t_use_t"])
    sx = None
    for kk in range(NK):
        sx = E.op("s", lambda e, kk=kk: e.activation(
            a2t[:, kk * TS:kk * TS + 8], tp_ps[0][:, kk * 8:(kk + 1) * 8],
            AF.Copy, scale=normw_sb[:, kk:kk + 1]), inc=("sS", 1))
    state["tp_war"][0] = sx
    E.wait("t", "sS", sx)
    pvl = None
    for j in range(NJ):
        if pvl is not None:
            E.wait("t", "vS", pvl)
        mml = None
        for kk in range(NK):
            mml = E.op("t", lambda e, j=j, kk=kk: e.matmul(
                pad_ps[:, 0:8],
                win_x[:, kk * D_INNER + j * 128:kk * D_INNER + (j + 1) * 128],
                a2t[:, kk * TS:kk * TS + 8],
                start=(kk == 0), stop=(kk == NK - 1)), inc=("tS", 1))
        E.wait("v", "tS", mml)
        pvl = E.op("v", lambda e, j=j: e.tensor_copy(
            xp_sb[:, j * FFW:j * FFW + 3], pad_ps[:, 5:8]), inc=("vS", 1))
    state["a2t_use_t"] = E.cnt["tS"]

    for k in range(NSUB_OWN):
        mamba_subchunk(k, own=True, xsrc=xown, xoff=8 + k * TS, mask_off=None,
                       skip_tail=(k == 0))


    E.wait("t", "vS", E.cnt["vS"])
    E.wait("t", "sS", E.cnt["sS"])
    E.wait("t", "dA", E.cnt["dA"])
    fence_t = E.op("t", lambda e: e.nop(), inc=("tS", 1))
    for qn in ("v", "s", "g", "sy"):
        E.wait(qn, "tS", fence_t)
    es1.close()

    es2 = ExitStack()
    sb2 = lambda n, sh, dt=F32: es2.enter_context(nc.sbuf_tensor(n, sh, dt))
    ps2 = lambda n, sh, dt=F32: es2.enter_context(nc.psum_tensor(n, sh, dt))

    x2_sb = sb2("x2_sb", [128, 4 * D_MODEL], F32)
    a22_sb = sb2("a22_sb", [128, 4 * D_MODEL], BF16)
    a1t = sb2("a1t", [128, NK * T_OWN], BF16)
    gbuf = sb2("gbuf", [128, 2 * TS], F32)
    hbst = sb2("hbst", [128, 2 * 8 * TS], BF16)
    w1p = sb2("w1p", [128, 2 * NK * 128], BF16)
    v1p = sb2("v1p", [128, 2 * NK * 128], BF16)
    tp2_ps = [ps2(f"tp2_ps{i}", [128, 1024], BF16) for i in range(2)]
    h_ps = [ps2(f"h_ps{i}", [128, TS], F32) for i in range(4)]
    r_ps = ps2("r_ps", [128, 512], F32)

    tp2_war = [None, None]
    x2_use = 0
    for k in range(NSUB_OWN):
        E.wait("sy", "vS", x2_use if x2_use else 0)
        d_x = E.dma("sy", x2_sb[:, :].rearrange("p (i d) -> p i d", i=4),
                    xown.ap()[8 + k * TS:8 + (k + 1) * TS, :]
                    .rearrange("(i p) d -> p i d", p=128), "dS")
        E.wait("v", "dS", d_x)
        sa = None
        for i in range(4):
            tt = k * 4 + i
            sa = E.op("v", lambda e, i=i, tt=tt: e.tensor_scalar(
                out=a22_sb[:, i * D_MODEL:(i + 1) * D_MODEL],
                in0=x2_sb[:, i * D_MODEL:(i + 1) * D_MODEL],
                scalar1=s1_all[:, tt:tt + 1], scalar2=b1_all[:, tt:tt + 1],
                op0=ALU.mult, op1=ALU.add), inc=("vS", 1))
        x2_use = sa
        E.wait("t", "vS", sa)
        for kk in range(NK):
            if tp2_war[kk % 2] is not None:
                E.wait("t", "sS", tp2_war[kk % 2])
            tl = None
            for i in range(4):
                tl = E.op("t", lambda e, kk=kk, i=i: e.transpose(
                    tp2_ps[kk % 2][:, i * 128:(i + 1) * 128],
                    a22_sb[:, i * D_MODEL + kk * 128:i * D_MODEL + (kk + 1) * 128],
                    ident[:, :]), inc=("tS", 1))
            E.wait("s", "tS", tl)
            sev = E.op("s", lambda e, kk=kk, k=k: e.activation(
                a1t[:, kk * T_OWN + k * TS:kk * T_OWN + (k + 1) * TS],
                tp2_ps[kk % 2][:, 0:512], AF.Copy), inc=("sS", 1))
            tp2_war[kk % 2] = sev
    a1t_ready = E.cnt["sS"]

    E.wait("t", "sS", a1t_ready)
    rp_use = 0
    for g4 in range(NT_OWN // 4):
        E.wait("t", "vS", rp_use)
        mml = None
        for i in range(4):
            tt = g4 * 4 + i
            for kk in range(NK):
                mml = E.op("t", lambda e, tt=tt, kk=kk, i=i: e.matmul(
                    r_ps[:, i * NE:(i + 1) * NE],
                    a1t[:, kk * T_OWN + tt * 128:kk * T_OWN + (tt + 1) * 128],
                    rw_sb[:, kk * NE:(kk + 1) * NE],
                    start=(kk == 0), stop=(kk == NK - 1)), inc=("tS", 1))
        E.wait("v", "tS", mml)
        rp_use = E.op("v", lambda e, g4=g4: e.tensor_copy(
            sc_sb[:, g4 * 4 * NE:(g4 + 1) * 4 * NE], r_ps[:, 0:4 * NE]),
            inc=("vS", 1))
    for tt in range(NT_OWN):
        p = sc_sb[:, tt * NE:(tt + 1) * NE]
        mx = st[:, 0:1]
        E.vstep(lambda e, p=p, mx=mx: e.reduce_max(mx, p, axis=AX.X))
        E.vstep(lambda e, p=p, mx=mx: e.tensor_scalar(
            out=p, in0=p, scalar1=mx, scalar2=None, op0=ALU.subtract))
    E.wait("s", "vS", E.cnt["vS"])
    sxx = E.op("s", lambda e: e.activation(sc_sb[:, 0:NT_OWN * NE],
                                           sc_sb[:, 0:NT_OWN * NE], AF.Exp),
               inc=("sS", 1))
    E.wait("s", "sS", sxx)
    E.wait("v", "sS", sxx)
    for tt in range(NT_OWN):
        p = sc_sb[:, tt * NE:(tt + 1) * NE]
        m = m_sb[:, tt * NE:(tt + 1) * NE]
        sm = st[:, 1:2]
        m1 = st[:, 2:3]
        m2 = st[:, 3:4]
        E.vstep(lambda e, p=p, sm=sm: e.reduce_sum(sm, p, axis=AX.X))
        E.vstep(lambda e, sm=sm: e.reciprocal(sm, sm))
        E.vstep(lambda e, p=p, sm=sm: e.tensor_scalar(
            out=p, in0=p, scalar1=sm, scalar2=None, op0=ALU.mult))
        E.vstep(lambda e, p=p, m1=m1: e.reduce_max(m1, p, axis=AX.X))
        E.vstep(lambda e, p=p, m=m, m1=m1: e.tensor_scalar(
            out=m, in0=p, scalar1=m1, scalar2=1e30, op0=ALU.is_ge, op1=ALU.mult))
        E.vstep(lambda e, p=p, m=m: e.tensor_tensor(m, p, m, op=ALU.subtract))
        E.vstep(lambda e, m=m, m2=m2: e.reduce_max(m2, m, axis=AX.X))
        E.vstep(lambda e, p=p, m=m, m2=m2: e.tensor_scalar(
            out=m, in0=p, scalar1=m2, scalar2=None, op0=ALU.is_ge))
        E.vstep(lambda e, p=p, m=m: e.tensor_tensor(m, p, m, op=ALU.mult))

    if dbg:
        E.wait("s", "vS", E.cnt["vS"])
        E.dma("s", dbg_t["dbg_m"].ap()[:, 0:NT_OWN * NE], m_sb[:, :], "dA")
        E.dma("s", dbg_t["dbg_sc"].ap()[:, 0:NT_OWN * NE], sc_sb[:, :], "dA")
        E.dma("s", dbg_t["dbg_s1"].ap()[:, 0:NT_OWN], s1_all[:, :], "dA")
        E.dma("s", dbg_t["dbg_b1"].ap()[:, 0:NT_OWN], b1_all[:, :], "dA")
        E.dma("s", dbg_t["dbg_a1t"].ap(), a1t[:, 0:NK * TS], "dA")
        dbg_done = E.cnt["dA"]
        for qn in ("v", "g", "sy", "t"):
            E.wait(qn, "dA", dbg_done)
        E.wait("s", "dA", dbg_done)
    E.wait("t", "vS", E.cnt["vS"])
    panel_use = {}
    hps_war_v = [None, None]
    hps_war_s = [None, None]
    hbst_dma = [0, 0]
    for ee in range(NE):
        for fb in range(NF):
            it = ee * NF + fb
            pp = it % 2
            E.wait("g", "tS", panel_use.get(pp, 0))
            E.dma("g", w1p[:, pp * NK * 128:(pp + 1) * NK * 128]
                  .rearrange("p (k f) -> p k f", k=NK),
                  w1.ap()[ee, :, fb * 128:(fb + 1) * 128]
                  .rearrange("(k p) f -> p k f", p=128), "dG")
            d_w1v = E.cnt["dG"]
            d_v1 = E.dma("g", v1p[:, pp * NK * 128:(pp + 1) * NK * 128]
                         .rearrange("p (k f) -> p k f", k=NK),
                         v1.ap()[ee, :, fb * 128:(fb + 1) * 128]
                         .rearrange("(k p) f -> p k f", p=128), "dV")
            E.wait("t", "dG", d_w1v)
            E.wait("t", "dV", d_v1)
            hoff = (it % 2) * 8 * TS
            E.wait("v", f"dH{it % 2}", hbst_dma[it % 2])
            mml = None
            for tsb in range(T_OWN // TS):
                pb = tsb % 2
                if hps_war_s[pb] is not None:
                    E.wait("t", "sS", hps_war_s[pb])
                    E.wait("t", "vS", hps_war_v[pb])
                for kk in range(NK):
                    E.op("t", lambda e, pp=pp, kk=kk, tsb=tsb, pb=pb: e.matmul(
                        h_ps[2 * pb][:, :],
                        w1p[:, pp * NK * 128 + kk * 128:pp * NK * 128 + (kk + 1) * 128],
                        a1t[:, kk * T_OWN + tsb * TS:kk * T_OWN + (tsb + 1) * TS],
                        start=(kk == 0), stop=(kk == NK - 1)),
                        inc=("tS", 1) if kk == NK - 1 else None)
                m1l = E.cnt["tS"]
                for kk in range(NK):
                    E.op("t", lambda e, pp=pp, kk=kk, tsb=tsb, pb=pb: e.matmul(
                        h_ps[2 * pb + 1][:, :],
                        v1p[:, pp * NK * 128 + kk * 128:pp * NK * 128 + (kk + 1) * 128],
                        a1t[:, kk * T_OWN + tsb * TS:kk * T_OWN + (tsb + 1) * TS],
                        start=(kk == 0), stop=(kk == NK - 1)),
                        inc=("tS", 1) if kk == NK - 1 else None)
                m2l = E.cnt["tS"]
                mml = m2l
                E.wait("s", "tS", m1l)
                if hps_war_v[pb] is not None:
                    E.wait("s", "vS", hps_war_v[pb])
                sg = E.op("s", lambda e, pb=pb: e.activation(
                    gbuf[:, pb * TS:(pb + 1) * TS], h_ps[2 * pb][:, :], AF.Gelu),
                    inc=("sS", 1))
                hps_war_s[pb] = sg
                E.wait("v", "tS", m2l)
                E.wait("v", "sS", sg)
                hv = E.op("v", lambda e, pb=pb, tsb=tsb, hoff=hoff: e.tensor_tensor(
                    hbst[:, hoff + tsb * TS:hoff + (tsb + 1) * TS],
                    gbuf[:, pb * TS:(pb + 1) * TS], h_ps[2 * pb + 1][:, :],
                    op=ALU.mult), inc=("vS", 1))
                hps_war_v[pb] = hv
            panel_use[pp] = mml
            E.wait("s", "vS", E.cnt["vS"])
            hbst_dma[it % 2] = E.dma("s", hb_scr.ap()[ee, fb, :, :],
                                     hbst[:, hoff:hoff + T_OWN],
                                     f"dH{it % 2}")

    E.wait("t", "vS", E.cnt["vS"])
    E.wait("t", "sS", E.cnt["sS"])
    E.wait("t", "dA", E.cnt["dA"])
    fence_t2 = E.op("t", lambda e: e.nop(), inc=("tS", 1))
    for qn in ("v", "s", "g", "sy"):
        E.wait(qn, "tS", fence_t2)
    es2.close()

    es3 = ExitStack()
    sb3 = lambda n, sh, dt=F32: es3.enter_context(nc.sbuf_tensor(n, sh, dt))
    ps3 = lambda n, sh, dt=F32: es3.enter_context(nc.psum_tensor(n, sh, dt))

    out_acc = sb3("out_acc", [128, NT_OWN * D_MODEL], F32)
    bias_bc = sb3("bias_bc", [128, D_MODEL], F32)
    s1p_all = sb3("s1p_all", [128, NT_OWN], F32)
    o_ps = [ps3(f"o_ps{i}", [128, 512], F32) for i in range(4)]

    es3a = ExitStack()
    x3_sb = es3a.enter_context(nc.sbuf_tensor("x3_sb", [128, 4 * D_MODEL], F32))
    ygt = es3a.enter_context(nc.sbuf_tensor("ygt", [128, NJ * TS], BF16))
    wout_sb = es3a.enter_context(nc.sbuf_tensor("wout_sb", [128, NJ * D_MODEL], BF16))

    E.vstep(lambda e: e.tensor_scalar(out=s1p_all[:, :], in0=s1_all[:, :],
                                      scalar1=1.0, scalar2=None, op0=ALU.add))
    d_wo = E.dma("g", wout_sb[:, :].rearrange("p (k c) -> p k c", k=NJ),
                 wout.ap().rearrange("(k p) c -> p k c", p=128), "dG")
    d_bb = E.dma("g", bias_bc[:, :],
                 fbias.ap()[0:1, :].broadcast_to([128, D_MODEL]), "dG")
    E.wait("t", "dG", d_wo)
    ygt_use = 0
    ops_use = [0, 0, 0, 0]
    for qd in range(T_OWN // TS):
        E.wait("sy", "vS", E.cnt["vS"])
        d_x = E.dma("sy", x3_sb[:, :].rearrange("p (i d) -> p i d", i=4),
                    xown.ap()[8 + qd * TS:8 + (qd + 1) * TS, :]
                    .rearrange("(i p) d -> p i d", p=128), "dS")
        dy = E.dma("sy", ygt[:, :].rearrange("p (j t) -> p j t", j=NJ),
                   ygb_scr.ap()[:, :, qd * TS:(qd + 1) * TS]
                   .rearrange("j p t -> p j t"), "dS")
        E.wait("v", "dS", d_x)
        E.wait("v", "dG", d_bb)
        for i in range(4):
            tt = qd * 4 + i
            E.op("v", lambda e, i=i, tt=tt: e.tensor_scalar(
                out=out_acc[:, tt * D_MODEL:(tt + 1) * D_MODEL],
                in0=x3_sb[:, i * D_MODEL:(i + 1) * D_MODEL],
                scalar1=s1p_all[:, tt:tt + 1], scalar2=b1_all[:, tt:tt + 1],
                op0=ALU.mult, op1=ALU.add))
            E.op("v", lambda e, i=i, tt=tt: e.tensor_tensor(
                out_acc[:, tt * D_MODEL:(tt + 1) * D_MODEL],
                out_acc[:, tt * D_MODEL:(tt + 1) * D_MODEL], bias_bc[:, :],
                op=ALU.add))
        vinit = E.op("v", lambda e: e.engine_nop(), inc=("vS", 1))
        E.wait("t", "dS", dy)
        E.wait("t", "vS", vinit)
        for i in range(4):
            tt = qd * 4 + i
            for half in range(2):
                pidx = 2 * (i % 2) + half
                E.wait("t", "vS", ops_use[pidx])
                mml = None
                for j in range(NJ):
                    mml = E.op("t", lambda e, j=j, i=i, half=half, pidx=pidx: e.matmul(
                        o_ps[pidx][:, :],
                        ygt[:, j * TS + i * 128:j * TS + (i + 1) * 128],
                        wout_sb[:, j * D_MODEL + half * 512:
                                j * D_MODEL + (half + 1) * 512],
                        start=(j == 0), stop=(j == NJ - 1)), inc=("tS", 1))
                E.wait("v", "tS", mml)
                vv = E.op("v", lambda e, tt=tt, half=half, pidx=pidx: e.tensor_tensor(
                    out_acc[:, tt * D_MODEL + half * 512:
                            tt * D_MODEL + (half + 1) * 512],
                    out_acc[:, tt * D_MODEL + half * 512:
                            tt * D_MODEL + (half + 1) * 512],
                    o_ps[pidx][:, :], op=ALU.add), inc=("vS", 1))
                ops_use[pidx] = vv
    if dbg:
        E.wait("s", "vS", E.cnt["vS"])
        E.dma("s", dbg_t["dbg_oi"].ap(), out_acc[:, 0:4 * D_MODEL], "dA")
        dbg_done = E.cnt["dA"]
        for qn in ("v", "g", "sy", "t"):
            E.wait(qn, "dA", dbg_done)
        E.wait("s", "dA", dbg_done)
    E.wait("t", "vS", E.cnt["vS"])
    fence_t3 = E.op("t", lambda e: e.nop(), inc=("tS", 1))
    for qn in ("v", "s", "g", "sy"):
        E.wait(qn, "tS", fence_t3)
    es3a.close()

    w2_sb = sb3("w2_sb", [128, NF * D_MODEL], BF16)
    hbq = sb3("hbq", [128, 2 * NF * 256], BF16)
    w2_use = 0
    hbq_use = [0, 0]
    for ee in range(NE):
        E.wait("g", "tS", w2_use)
        d_w2 = E.dma("g", w2_sb[:, :].rearrange("p (f c) -> p f c", f=NF),
                     w2.ap()[ee, :, :].rearrange("(f p) c -> p f c", p=128), "dG")
        E.wait("t", "dG", d_w2)
        for ch in range(T_OWN // 256):
            pb = ch % 2
            E.wait("sy", "tS", hbq_use[pb])
            d_hb = E.dma("sy",
                         hbq[:, pb * NF * 256:(pb + 1) * NF * 256]
                         .rearrange("p (f t) -> p f t", f=NF),
                         hb_scr.ap()[ee, :, :, ch * 256:(ch + 1) * 256]
                         .rearrange("f p t -> p f t"), "dS")
            E.wait("t", "dS", d_hb)
            for i2 in range(2):
                tt = ch * 2 + i2
                for half in range(2):
                    pidx = 2 * i2 + half
                    E.wait("t", "vS", ops_use[pidx])
                    mml = None
                    for fb in range(NF):
                        mml = E.op("t", lambda e, fb=fb, i2=i2, half=half, pb=pb,
                                   pidx=pidx: e.matmul(
                            o_ps[pidx][:, :],
                            hbq[:, pb * NF * 256 + fb * 256 + i2 * 128:
                                pb * NF * 256 + fb * 256 + (i2 + 1) * 128],
                            w2_sb[:, fb * D_MODEL + half * 512:
                                  fb * D_MODEL + (half + 1) * 512],
                            start=(fb == 0), stop=(fb == NF - 1)), inc=("tS", 1))
                    E.wait("v", "tS", mml)
                    vv = E.op("v", lambda e, tt=tt, half=half, ee=ee, pidx=pidx:
                              e.scalar_tensor_tensor(
                        out_acc[:, tt * D_MODEL + half * 512:
                                tt * D_MODEL + (half + 1) * 512],
                        o_ps[pidx][:, :],
                        m_sb[:, tt * NE + ee:tt * NE + ee + 1],
                        out_acc[:, tt * D_MODEL + half * 512:
                                tt * D_MODEL + (half + 1) * 512],
                        op0=ALU.mult, op1=ALU.add), inc=("vS", 1))
                    ops_use[pidx] = vv
            hbq_use[pb] = E.cnt["tS"]
        w2_use = E.cnt["tS"]

    E.wait("s", "vS", E.cnt["vS"])
    for tt in range(NT_OWN):
        E.dma("s", out_t.ap()[tt * 128:(tt + 1) * 128, :],
              out_acc[:, tt * D_MODEL:(tt + 1) * D_MODEL], "dA")
    E.wait("s", "dA", E.cnt["dA"])
    E.wait("g", "dA", E.cnt["dA"])
    E.wait("sy", "dS", E.cnt["dS"])
    E.wait("sy", "dA", E.cnt["dA"])

    with nc.Block() as block:
        @block.tensor
        def _(eng):
            E.replay(eng, "t")

        @block.vector
        def _(eng):
            E.replay(eng, "v")

        @block.scalar
        def _(eng):
            E.replay(eng, "s")

        @block.gpsimd
        def _(eng):
            E.replay(eng, "g")

        @block.sync
        def _(eng):
            E.replay(eng, "sy")

    es3.close()
    es0.close()
    return nc


def _tile_rows(a, nt):
    a2 = np.asarray(a, dtype=np.float32).reshape(nt, 128, -1)
    return np.ascontiguousarray(a2.transpose(1, 0, 2).reshape(128, -1))


def stage_inputs(cfg, x, norm_w, W_in, conv_w, conv_b, W_x, w_dt, b_dt, A_log,
                 D_param, W_out, router_w, w1, v1, w2, bias):
    T_PRE, T_OWN = cfg["T_PRE"], cfg["T_OWN"]
    x = np.asarray(x, dtype=np.float32)
    B, T, D = x.shape
    f32 = lambda a: np.ascontiguousarray(np.asarray(a, dtype=np.float32))
    base = {
        "win": f32(W_in), "wout": f32(W_out),
        "wx_r": _tile_rows(W_x, NJ), "alog_r": _tile_rows(A_log, NJ),
        "wdt_r": _tile_rows(np.reshape(w_dt, (-1, 1)), NJ),
        "bdt_r": _tile_rows(np.reshape(b_dt, (-1, 1)), NJ),
        "cw_r": _tile_rows(np.reshape(conv_w, (D_INNER, 4)), NJ),
        "cb_r": _tile_rows(np.reshape(conv_b, (-1, 1)), NJ),
        "dD_r": _tile_rows(np.reshape(D_param, (-1, 1)), NJ),
        "normw_r": _tile_rows(np.reshape(norm_w, (-1, 1)), NK),
        "rw_r": _tile_rows(router_w, NK),
        "w1": f32(w1), "v1": f32(v1), "w2": f32(w2),
        "fbias": f32(bias).reshape(1, D_MODEL),
        "ident_in": np.eye(128, dtype=np.float32).astype(ml_dtypes.bfloat16),
    }
    in_maps = []
    for c in range(8):
        b, q = c // 4, c % 4
        t0 = q * T_OWN
        mask = np.zeros((1, max(T_PRE, 1)), dtype=np.float32)
        mask[0, :min(t0, T_PRE)] = -1.0
        prev8 = x[b, t0 - 8:t0] if t0 >= 8 else np.zeros((8, D), dtype=np.float32)
        xo = np.ascontiguousarray(np.concatenate([prev8, x[b, t0:t0 + T_OWN]], axis=0))
        m = dict(base)
        m["xpre"] = np.ascontiguousarray(x[b, :T_PRE]) if T_PRE > 0 else \
            np.zeros((1, D), dtype=np.float32)
        m["xown"] = xo
        m["maskpre"] = mask
        in_maps.append(m)
    return in_maps


def kernel(L0_attn_x, L0_attn_norm_w, L0_attn_W_in, L0_attn_conv_w, L0_attn_conv_b,
           L0_attn_W_x, L0_attn_w_dt, L0_attn_b_dt, L0_attn_A_log, L0_attn_D_param,
           L0_attn_W_out, L0_ffn_router_w, L0_ffn_w1, L0_ffn_v1, L0_ffn_w2,
           L0_ffn_bias):
    cfg = default_cfg()
    x = np.asarray(L0_attn_x, dtype=np.float32)
    B, T, D = x.shape
    nc = build_program(cfg)
    in_maps = stage_inputs(cfg, x, L0_attn_norm_w, L0_attn_W_in, L0_attn_conv_w,
                           L0_attn_conv_b, L0_attn_W_x, L0_attn_w_dt, L0_attn_b_dt,
                           L0_attn_A_log, L0_attn_D_param, L0_attn_W_out,
                           L0_ffn_router_w, L0_ffn_w1, L0_ffn_v1, L0_ffn_w2,
                           L0_ffn_bias)
    res = run_bass_kernel_spmd(nc, in_maps, core_ids=list(range(8)))
    T_OWN = cfg["T_OWN"]
    out = np.empty((B, T, D), dtype=np.float32)
    for c in range(8):
        b, q = c // 4, c % 4
        out[b, q * T_OWN:(q + 1) * T_OWN] = res.results[c]["out"]
    return out
